# version 35
# baseline (speedup 1.0000x reference)
import numpy as np
import concourse.bass as bass
import concourse.mybir as mybir
from concourse.bass_utils import run_bass_kernel_spmd
from contextlib import ExitStack

F32 = mybir.dt.float32
BF16 = mybir.dt.bfloat16
I32 = mybir.dt.int32
AF = mybir.ActivationFunctionType
ALU = mybir.AluOpType
AX = mybir.AxisListType

QCH = 512
BCH = 256

CFG_FULL = dict(B=4, T=2048, C=1024, NH=16, HS=64, L=512, QL=512, H=3072)


class Prog:

    STREAMS = ("tensor", "vector", "scalar", "gpsimd", "sync")
    QSTREAM = {"qsync": "sync", "qscal": "scalar", "qgps": "gpsimd"}
    NSLOT = 8

    def __init__(self):
        self.items = {s: [] for s in self.STREAMS}
        self.cnt = {s: 0 for s in self.STREAMS}
        self.qslot = {q: 0 for q in self.QSTREAM}
        for q in self.QSTREAM:
            for i in range(self.NSLOT):
                self.cnt[q + str(i)] = 0
        self.res = {}
        self.satisfied = {s: {} for s in self.STREAMS}
        self.max_waited = {}
        self.n_ops = 0

    def sem_names(self):
        names = list(self.STREAMS)
        for q in self.QSTREAM:
            names += [q + str(i) for i in range(self.NSLOT)]
        return names

    def _wait(self, stream, tok):
        sem, val = tok
        if sem == stream == "tensor":
            return
        if self.satisfied[stream].get(sem, 0) >= val:
            return
        self.items[stream].append(("wait", sem, val))
        self.satisfied[stream][sem] = val
        v = self.max_waited.get(sem, 0)
        if val > v:
            self.max_waited[sem] = val

    def _deps(self, stream, reads, writes):
        toks = []
        for r in reads:
            e = self.res.get(r)
            if e and e[0] is not None:
                toks.append(e[0])
        for w in writes:
            e = self.res.get(w)
            if e:
                if e[0] is not None:
                    toks.append(e[0])
                toks.extend(e[1])
        for t in sorted(set(toks)):
            self._wait(stream, t)

    def _mark(self, tok, reads, writes):
        for w in writes:
            self.res[w] = [tok, []]
        for r in reads:
            e = self.res.setdefault(r, [None, []])
            e[1].append(tok)

    def op(self, stream, fn, reads=(), writes=(), inc=True):
        self._deps(stream, reads, writes)
        if inc:
            self.cnt[stream] += 1
            tok = (stream, self.cnt[stream])
            self.items[stream].append(("op", fn, stream, 1))
        else:
            tok = (stream, self.cnt[stream] + 1)
            self.items[stream].append(("op", fn, None, 0))
        self._mark(tok, reads, writes)
        self.n_ops += 1

    def dma(self, queue, out_ap, in_ap, reads=(), writes=()):
        stream = self.QSTREAM[queue]
        slot = self.qslot[queue]
        self.qslot[queue] = (slot + 1) % self.NSLOT
        sem = queue + str(slot)
        self._wait(stream, (sem, self.cnt[sem]))
        self._deps(stream, reads, writes)
        self.cnt[sem] += 16
        tok = (sem, self.cnt[sem])
        def fn(eng, o=out_ap, i=in_ap):
            return eng.dma_start(out=o, in_=i)
        self.items[stream].append(("op", fn, sem, 16))
        self._mark(tok, reads, writes)
        self.n_ops += 1

    def barrier(self):
        snap = [(sem, val) for sem, val in self.cnt.items() if val > 0]
        for s in self.STREAMS:
            for sem, val in snap:
                self._wait(s, (sem, val))

    def finalize_check(self):
        for sem, val in self.max_waited.items():
            assert val <= self.cnt[sem], (
                f"wait on {sem} >= {val} but final count is {self.cnt[sem]} (hang)")

    def emit(self, block, sems):
        self.finalize_check()
        def mk(stream):
            def body(eng):
                for it in self.items[stream]:
                    if it[0] == "wait":
                        eng.wait_ge(sems[it[1]], it[2])
                    else:
                        inst = it[1](eng)
                        if it[2] is not None:
                            inst.then_inc(sems[it[2]], it[3])
            return body
        for stream in self.STREAMS:
            getattr(block, stream)(mk(stream))


def build(nc, cfg):
    B, T, C, NH, HS = cfg["B"], cfg["T"], cfg["C"], cfg["NH"], cfg["HS"]
    L, QL, H = cfg["L"], cfg["QL"], cfg["H"]
    CT, LT, QLT, HT, TT = C // 128, L // 128, QL // 128, H // 128, T // 128
    NQC = T // QCH
    NKP = QCH // 128
    NBC = B * T // BCH
    SW = 1.0 / 127.0
    def chunks(total, size=512):
        return [(o, min(size, total - o)) for o in range(0, total, size)]
    CCH = chunks(C)
    TCH = chunks(T)
    NCH = len(CCH)
    HPT = 128 // HS

    x_d = nc.dram_tensor("x", [B, T, C], F32, kind="ExternalInput").ap()
    Wdq_d = nc.dram_tensor("W_dq", [QL, C], F32, kind="ExternalInput").ap()
    Wuq_d = nc.dram_tensor("W_uq", [C, QL], F32, kind="ExternalInput").ap()
    Wdkv_d = nc.dram_tensor("W_dkv", [L, C], F32, kind="ExternalInput").ap()
    Wuk_d = nc.dram_tensor("W_uk", [C, L], F32, kind="ExternalInput").ap()
    Wuv_d = nc.dram_tensor("W_uv", [C, L], F32, kind="ExternalInput").ap()
    Wo_d = nc.dram_tensor("W_o", [C, C], F32, kind="ExternalInput").ap()
    wg_d = nc.dram_tensor("w_gate", [H, C], I32, kind="ExternalInput").ap()
    wu_d = nc.dram_tensor("w_up", [H, C], I32, kind="ExternalInput").ap()
    wd_d = nc.dram_tensor("w_down", [C, H], I32, kind="ExternalInput").ap()
    out_d = nc.dram_tensor("out", [B, T, C], F32, kind="ExternalOutput").ap()
    xn_s = nc.dram_tensor("xn_s", [B, CT, 128, T], BF16, kind="Internal").ap()
    g_s = nc.dram_tensor("g_s", [B, TT, 128, C], BF16, kind="Internal").ap()
    keff_s = nc.dram_tensor("keff_s", [128, CT, L], BF16, kind="Internal").ap()
    veff_s = nc.dram_tensor("veff_s", [128, LT, C], BF16, kind="Internal").ap()
    WuqT_s = nc.dram_tensor("WuqT_s", [128, QLT, C], BF16, kind="Internal").ap()
    WdqT_s = nc.dram_tensor("WdqT_s", [128, CT, QL], BF16, kind="Internal").ap()
    WdkvT_s = nc.dram_tensor("WdkvT_s", [128, CT, L], BF16, kind="Internal").ap()

    P = Prog()

    with ExitStack() as top:
        ent = top.enter_context
        sems = {}
        for name in P.sem_names():
            sems[name] = ent(nc.semaphore("sem_" + name))

        ident_b = ent(nc.sbuf_tensor("ident_b", [128, 128], BF16))
        ident_f = ent(nc.sbuf_tensor("ident_f", [128, 128], F32))
        negtri = ent(nc.sbuf_tensor("negtri", [128, QCH], BF16))
        ones_b = ent(nc.sbuf_tensor("ones_b", [128, 4], BF16))
        ones_f = ent(nc.sbuf_tensor("ones_f", [128, 4], F32))
        ones_m = ent(nc.sbuf_tensor("ones_m", [128, 128], BF16))
        iota_f = ent(nc.sbuf_tensor("iota_f", [128, QCH], F32))
        stat = ent(nc.sbuf_tensor("stat", [128, 16], F32))

        psum = [ent(nc.psum_tensor("ps%d" % i, [128, 512], F32)) for i in range(8)]
        PSK = ["PS%d" % i for i in range(8)]

        def psum_bf(i):
            return psum[i][:].bitcast(BF16)

        def mk_iota(dst, n, base):
            def fn(eng, d=dst, nn=n, bb=base):
                return eng.iota(d, [[1, nn]], base=bb, channel_multiplier=-1,
                                allow_small_or_imprecise_dtypes=True)
            return fn
        P.op("gpsimd", mk_iota(iota_f[:, 0:128], 128, 0), writes=[("io",)])
        P.op("vector", lambda v: v.tensor_single_scalar(ident_b[:], iota_f[:, 0:128], 0.0, ALU.is_equal),
             reads=[("io",)], writes=[("identb",)])
        P.op("vector", lambda v: v.tensor_single_scalar(ident_f[:], iota_f[:, 0:128], 0.0, ALU.is_equal),
             reads=[("io",)], writes=[("identf",)])
        P.op("gpsimd", mk_iota(iota_f[:], QCH, 0), writes=[("io",)])
        P.op("vector", lambda v: v.tensor_single_scalar(negtri[:], iota_f[:], 0.0, ALU.is_lt),
             reads=[("io",)], writes=[("negtri",)])
        P.op("vector", lambda v: v.tensor_scalar(negtri[:], negtri[:], -100000.0, None, ALU.mult),
             reads=[("negtri",)], writes=[("negtri",)])
        P.op("vector", lambda v: v.memset(ones_b[:], 1.0), writes=[("ones",)])
        P.op("vector", lambda v: v.memset(ones_f[:], 1.0), writes=[("ones",)])
        P.op("vector", lambda v: v.memset(ones_m[:], 1.0), writes=[("ones",)])

        with ExitStack() as ph0:
            e0 = ph0.enter_context
            Wuq_f = e0(nc.sbuf_tensor("Wuq_f", [128, CT, QL], F32))
            Wuk_f = e0(nc.sbuf_tensor("Wuk_f", [128, CT, L], F32))
            Wdq_f = e0(nc.sbuf_tensor("Wdq_f", [128, QLT, C], F32))
            Wdkv_f = e0(nc.sbuf_tensor("Wdkv_f", [128, LT, C], F32))
            Wuv_f = e0(nc.sbuf_tensor("Wuv_f", [128, CT, L], F32))
            Wo_f = e0(nc.sbuf_tensor("Wo_f", [128, CT, C], F32))
            WoT_f = e0(nc.sbuf_tensor("WoT_f", [128, CT, C], F32))
            tmp1 = e0(nc.sbuf_tensor("tmp1", [128, QLT, L], F32))
            stage_b = e0(nc.sbuf_tensor("stage_b", [128, 16, 1024], BF16))
            wevac = e0(nc.sbuf_tensor("wevac", [128, 1024], BF16))
            keff_sb = e0(nc.sbuf_tensor("keff_sb", [128, CT, L], BF16))
            veff_sb = e0(nc.sbuf_tensor("veff_sb", [128, LT, C], BF16))

            def wload(dst, src_view, key, ntile):
                hlf = max(1, ntile // 2)
                for hh, (a0, a1) in enumerate([(0, hlf), (hlf, ntile)]):
                    if a1 > a0:
                        P.dma("qsync", dst[:, a0:a1, :], src_view[:, a0:a1, :], writes=[key, (key[0], hh)])
            wload(Wuq_f, Wuq_d.rearrange("(a p) q -> p a q", p=128), ("Wuq_f",), CT)
            wload(Wuk_f, Wuk_d.rearrange("(a p) l -> p a l", p=128), ("Wuk_f",), CT)
            wload(Wdq_f, Wdq_d.rearrange("(a p) c -> p a c", p=128), ("Wdq_f",), QLT)
            wload(Wdkv_f, Wdkv_d.rearrange("(a p) c -> p a c", p=128), ("Wdkv_f",), LT)
            wload(Wuv_f, Wuv_d.rearrange("(a p) l -> p a l", p=128), ("Wuv_f",), CT)
            wload(Wo_f, Wo_d.rearrange("(a p) c -> p a c", p=128), ("Wo_f",), CT)

            for qlc in range(QLT):
                bank = qlc % 2
                def fn(t, qlc=qlc, bank=bank):
                    for ct in range(CT):
                        inst = t.matmul(psum[bank][:, 0:L], Wuq_f[:, ct, qlc * 128:(qlc + 1) * 128],
                                        Wuk_f[:, ct, 0:L], start=(ct == 0), stop=(ct == CT - 1))
                    return inst
                P.op("tensor", fn, reads=[("Wuq_f",), ("Wuk_f",)], writes=[PSK[bank]])
                P.op("scalar", lambda s, qlc=qlc, bank=bank: s.copy(tmp1[:, qlc, :], psum[bank][:, 0:L]),
                     reads=[PSK[bank]], writes=[("tmp1", qlc)])
            for cch in range(CT):
                bank = cch % 2
                def fn(t, cch=cch, bank=bank):
                    for qlt in range(QLT):
                        inst = t.matmul(psum[bank][:, 0:L], Wdq_f[:, qlt, cch * 128:(cch + 1) * 128],
                                        tmp1[:, qlt, :], start=(qlt == 0), stop=(qlt == QLT - 1))
                    return inst
                P.op("tensor", fn, reads=[("Wdq_f",)] + [("tmp1", q) for q in range(QLT)], writes=[PSK[bank]])
                P.op("scalar", lambda s, cch=cch, bank=bank: s.copy(keff_sb[:, cch, :], psum[bank][:, 0:L]),
                     reads=[PSK[bank]], writes=[("keff_sb",)])
            P.dma("qgps", keff_s, keff_sb[:], reads=[("keff_sb",)])

            def emit_T(src_f32, src_key, rows_t, cols_t, dst_dram, tag, row0=0):
                for rt in range(rows_t):
                    hh = 0 if rt < max(1, rows_t // 2) else 1
                    P.op("scalar", lambda s, rt=rt: s.copy(stage_b[:, row0 + rt, 0:cols_t * 128], src_f32[:, rt, :]),
                         reads=[(src_key[0], hh)], writes=[(tag + "stg", rt)])
                for cc in range(cols_t):
                    bank = cc % 2
                    def fn(t, cc=cc, bank=bank):
                        for rt in range(rows_t):
                            inst = t.transpose(psum_bf(bank)[:, rt * 128:(rt + 1) * 128],
                                               stage_b[:, row0 + rt, cc * 128:(cc + 1) * 128], ident_b[:])
                        return inst
                    P.op("tensor", fn, reads=[("identb",)] + [(tag + "stg", rt) for rt in range(rows_t)],
                         writes=[PSK[bank]])
                    P.op("scalar", lambda s, cc=cc, bank=bank: s.copy(wevac[:, 0:rows_t * 128],
                                                                      psum_bf(bank)[:, 0:rows_t * 128]),
                         reads=[PSK[bank]], writes=[("wevac",)])
                    P.dma("qgps", dst_dram[:, cc, :], wevac[:, 0:rows_t * 128], reads=[("wevac",)])

            emit_T(Wuq_f, ("Wuq_f",), CT, QLT, WuqT_s, "wuq", row0=0)
            emit_T(Wdq_f, ("Wdq_f",), QLT, CT, WdqT_s, "wdq", row0=CT)
            emit_T(Wdkv_f, ("Wdkv_f",), LT, CT, WdkvT_s, "wdkv", row0=CT + QLT)

            for cc in range(CT):
                for half in range((CT + 3) // 4):
                    nrt = min(4, CT - half * 4)
                    bank = (cc * 2 + half) % 2 + 2
                    def fn(t, cc=cc, half=half, bank=bank, nrt=nrt):
                        for rt in range(nrt):
                            rtg = half * 4 + rt
                            inst = t.transpose(psum[bank][:, rt * 128:(rt + 1) * 128],
                                               Wo_f[:, rtg, cc * 128:(cc + 1) * 128], ident_f[:])
                        return inst
                    wokeys = sorted({("Wo_f", 0 if (half * 4 + rt) < max(1, CT // 2) else 1)
                                     for rt in range(nrt)})
                    P.op("tensor", fn, reads=[("identf",)] + wokeys, writes=[PSK[bank]])
                    P.op("scalar", lambda s, cc=cc, half=half, bank=bank, nrt=nrt:
                         s.copy(WoT_f[:, cc, half * 512:half * 512 + nrt * 128], psum[bank][:, 0:nrt * 128]),
                         reads=[PSK[bank]], writes=[("WoT", cc)])
            for lc in range(LT):
                for ic, (co, cl) in enumerate(CCH):
                    bank = (lc + ic) % 2 + 4
                    def fn(t, lc=lc, co=co, cl=cl, bank=bank):
                        for ct in range(CT):
                            inst = t.matmul(psum[bank][:, 0:cl], Wuv_f[:, ct, lc * 128:(lc + 1) * 128],
                                            WoT_f[:, ct, co:co + cl],
                                            start=(ct == 0), stop=(ct == CT - 1))
                        return inst
                    P.op("tensor", fn, reads=[("Wuv_f",)] + [("WoT", c2) for c2 in range(CT)], writes=[PSK[bank]])
                    P.op("scalar", lambda s, lc=lc, co=co, cl=cl, bank=bank:
                         s.copy(veff_sb[:, lc, co:co + cl], psum[bank][:, 0:cl]),
                         reads=[PSK[bank]], writes=[("veff_sb",)])
            P.dma("qgps", veff_s, veff_sb[:], reads=[("veff_sb",)])

        P.barrier()

        with ExitStack() as phB:
            eB = phB.enter_context
            wgT = eB(nc.sbuf_tensor("wgT", [128, CT, H], BF16))
            wuT = eB(nc.sbuf_tensor("wuT", [128, CT, H], BF16))
            wdT = eB(nc.sbuf_tensor("wdT", [128, HT, C], BF16))
            wstg = eB(nc.sbuf_tensor("wstg", [128, 2, H], BF16))
            sq_dummy = eB(nc.sbuf_tensor("sq_dummy", [128, 1024], F32))
            x_c = [eB(nc.sbuf_tensor("x_c%d" % i, [128, BCH // 128, C], F32)) for i in range(2)]
            xn_tm = eB(nc.sbuf_tensor("xn_tm", [128, BCH // 128, C], BF16))
            xn_fm = [eB(nc.sbuf_tensor("xn_fm%d" % i, [128, CT, BCH], BF16)) for i in range(2)]
            p_sb = [eB(nc.sbuf_tensor("p_sb%d" % i, [128, BCH], BF16)) for i in range(2)]
            h_sb = [eB(nc.sbuf_tensor("h_sb%d" % i, [128, BCH], BF16)) for i in range(2)]
            g_sb = [eB(nc.sbuf_tensor("g_sb%d" % i, [128, BCH // 128, C], BF16)) for i in range(2)]

            def load_ffn(dst, src_d, tiles, tag):
                ctiles = dst.shape[1]
                for tile in range(tiles):
                    sp = tile % 2
                    sbuf = wstg[:, sp, 0:ctiles * 128]
                    P.dma("qgps", sbuf, src_d[tile * 128:(tile + 1) * 128, :],
                          reads=[("wstg", sp)], writes=[("wstg", sp)])
                    ngrp = (ctiles + 7) // 8
                    for grp in range(ngrp):
                        lo, hi = grp * 8, min(grp * 8 + 8, ctiles)
                        bank = (tile * ngrp + grp) % 2
                        def fn(t, lo=lo, hi=hi, bank=bank, sbuf=sbuf):
                            for k in range(hi - lo):
                                inst = t.transpose(psum_bf(bank)[:, k * 128:(k + 1) * 128],
                                                   sbuf[:, (lo + k) * 128:(lo + k + 1) * 128], ident_b[:])
                            return inst
                        P.op("tensor", fn, reads=[("identb",), ("wstg", sp)], writes=[PSK[bank]])
                        def ev(s, tile=tile, lo=lo, hi=hi, bank=bank, dst=dst):
                            return s.copy(dst[:, lo:hi, tile * 128:(tile + 1) * 128],
                                          psum_bf(bank)[:, 0:(hi - lo) * 128].rearrange("p (a c) -> p a c", c=128))
                        P.op("scalar", ev, reads=[PSK[bank]], writes=[(tag, tile)])

            def load_ffn_pair(dst, src_d, tag, tile0, ntile):
                ctiles = dst.shape[1]
                ncols = ctiles * 128
                sp = LF_CNT[0] % 2
                LF_CNT[0] += 1
                sbuf = wstg[:, sp, 0:ntile * ncols]
                def regions(c0, c1):
                    return [("wstg", sp, r) for r in range(c0 // C, (c1 + C - 1) // C)]
                for ti in range(ntile):
                    P.dma("qgps", sbuf[:, ti * ncols:(ti + 1) * ncols],
                          src_d[(tile0 + ti) * 128:(tile0 + ti + 1) * 128, :],
                          reads=regions(ti * ncols, (ti + 1) * ncols),
                          writes=regions(ti * ncols, (ti + 1) * ncols))
                ngrp = (ctiles + 7) // 8
                for ti in range(ntile):
                    tile = tile0 + ti
                    for grp in range(ngrp):
                        lo, hi = grp * 8, min(grp * 8 + 8, ctiles)
                        bank = LF_CNT[0] % 2
                        def fn(t, lo=lo, hi=hi, bank=bank, sbuf=sbuf, ti=ti):
                            for k in range(hi - lo):
                                inst = t.transpose(psum_bf(bank)[:, k * 128:(k + 1) * 128],
                                                   sbuf[:, ti * ncols + (lo + k) * 128: ti * ncols + (lo + k + 1) * 128],
                                                   ident_b[:])
                            return inst
                        P.op("tensor", fn,
                             reads=[("identb",)] + regions(ti * ncols + lo * 128, ti * ncols + hi * 128),
                             writes=[PSK[bank]])
                        def ev(s, tile=tile, lo=lo, hi=hi, bank=bank, dst=dst):
                            return s.copy(dst[:, lo:hi, tile * 128:(tile + 1) * 128],
                                          psum_bf(bank)[:, 0:(hi - lo) * 128].rearrange("p (a c) -> p a c", c=128))
                        P.op("scalar", ev, reads=[PSK[bank]], writes=[(tag, tile)])

            LF_CNT = [0]
            wd_next = [0]
            WG_NT = max(1, min(3, H // C))
            for tp in range(0, HT, WG_NT):
                load_ffn_pair(wgT, wg_d, "wgT", tp, min(WG_NT, HT - tp))
                load_ffn_pair(wuT, wu_d, "wuT", tp, min(WG_NT, HT - tp))
                if wd_next[0] < CT:
                    load_ffn_pair(wdT, wd_d, "wdT", wd_next[0], 1)
                    wd_next[0] += 1
            while wd_next[0] < CT:
                load_ffn_pair(wdT, wd_d, "wdT", wd_next[0], 1)
                wd_next[0] += 1

            NSUB = BCH // 128

            def ln_pieces(i):
                b, tc = i // (T // BCH), i % (T // BCH)
                t0 = tc * BCH
                par = i % 2
                xk = ("x_c", par)
                pieces = []
                def load():
                    P.dma("qsync" if i % 2 == 0 else "qscal", x_c[par][:],
                          x_d[b, t0:t0 + BCH, :].rearrange("(s p) c -> p s c", p=128),
                          writes=[xk])
                pieces.append(load)
                def stats_v(s_):
                    P.op("vector", lambda v, par=par, s_=s_: v.reduce_sum(stat[:, s_:s_ + 1], x_c[par][:, s_, :], axis=AX.X),
                         reads=[xk], writes=[("stat",)])
                def stats_s(s_):
                    P.op("scalar", lambda s, par=par, s_=s_: s.activation(sq_dummy[:, 0:C], x_c[par][:, s_, :], AF.Square,
                                                                          accum_out=stat[:, 2 + s_:3 + s_]),
                         reads=[xk], writes=[("stat",), ("sqd",)])
                for s_ in range(NSUB):
                    pieces.append(lambda s_=s_: stats_v(s_))
                    pieces.append(lambda s_=s_: stats_s(s_))
                def meanvar():
                    P.op("vector", lambda v: v.tensor_scalar(stat[:, 4:6], stat[:, 0:2], 1.0 / C, None, ALU.mult),
                         reads=[("stat",)], writes=[("stat",)])
                    P.op("vector", lambda v: v.tensor_tensor(stat[:, 6:8], stat[:, 4:6], stat[:, 4:6], ALU.mult),
                         reads=[("stat",)], writes=[("stat",)])
                    P.op("vector", lambda v: v.scalar_tensor_tensor(stat[:, 8:10], stat[:, 2:4], 1.0 / C, stat[:, 6:8],
                                                                    ALU.mult, ALU.subtract),
                         reads=[("stat",)], writes=[("stat",)])
                    P.op("vector", lambda v: v.tensor_scalar(stat[:, 8:10], stat[:, 8:10], 1e-5, None, ALU.add),
                         reads=[("stat",)], writes=[("stat",)])
                    P.op("scalar", lambda s: s.activation(stat[:, 10:12], stat[:, 8:10], AF.Sqrt),
                         reads=[("stat",)], writes=[("stat",)])
                    P.op("vector", lambda v: v.reciprocal(stat[:, 12:14], stat[:, 10:12]),
                         reads=[("stat",)], writes=[("stat",)])
                pieces.append(meanvar)
                def xn_one(s_, hh):
                    P.op("vector", lambda v, par=par, s_=s_, hh=hh: v.tensor_scalar(
                            xn_tm[:, s_, hh * (C // 2):(hh + 1) * (C // 2)],
                            x_c[par][:, s_, hh * (C // 2):(hh + 1) * (C // 2)],
                            stat[:, 4 + s_:5 + s_], stat[:, 12 + s_:13 + s_],
                            ALU.subtract, ALU.mult),
                         reads=[xk, ("stat",)], writes=[("xn_tm", s_)])
                for s_ in range(NSUB):
                    for hh in range(2):
                        pieces.append(lambda s_=s_, hh=hh: xn_one(s_, hh))
                def tr_one(s_):
                    bank = 6 + (s_ % 2)
                    def fn(t, s_=s_, bank=bank):
                        for ct in range(CT):
                            inst = t.transpose(psum_bf(bank)[:, ct * 128:(ct + 1) * 128],
                                               xn_tm[:, s_, ct * 128:(ct + 1) * 128], ident_b[:])
                        return inst
                    P.op("tensor", fn, reads=[("identb",), ("xn_tm", s_)], writes=[PSK[bank]])
                    def ev(v, s_=s_, bank=bank, par=par):
                        return v.tensor_copy(xn_fm[par][:, :, s_ * 128:(s_ + 1) * 128],
                                             psum_bf(bank)[:, 0:CT * 128].rearrange("p (a c) -> p a c", c=128))
                    P.op("vector", ev, reads=[PSK[bank]], writes=[("xn_fm", par)])
                for s_ in range(NSUB):
                    pieces.append(lambda s_=s_: tr_one(s_))
                def store():
                    P.dma("qgps", xn_s[b, :, :, t0:t0 + BCH].rearrange("a p t -> p a t"), xn_fm[par][:],
                          reads=[("xn_fm", par)])
                pieces.append(store)
                return pieces

            def emit_ffn(i, pend):
                b, tc = i // (T // BCH), i % (T // BCH)
                par = i % 2
                def emit_down(hc):
                    hp = hc % 2
                    if hc == 0:
                        for tt in range(NSUB):
                            for ic, (co, cl) in enumerate(CCH):
                                P.op("tensor", lambda t, hc=hc, hp=hp, tt=tt, co=co, cl=cl, ic=ic:
                                     t.matmul(psum[2 + tt * NCH + ic][:, 0:cl],
                                              h_sb[hp][:, tt * 128:(tt + 1) * 128],
                                              wdT[:, hc, co:co + cl],
                                              start=True, stop=(HT == 1)),
                                     reads=[("h", hp), ("wdT", hc)],
                                     writes=[PSK[2 + tt * NCH + ic]])
                        return
                    def fnd(t, hc=hc, hp=hp):
                        for tt in range(NSUB):
                            for ic, (co, cl) in enumerate(CCH):
                                inst = t.matmul(psum[2 + tt * NCH + ic][:, 0:cl],
                                                h_sb[hp][:, tt * 128:(tt + 1) * 128],
                                                wdT[:, hc, co:co + cl],
                                                start=False, stop=(hc == HT - 1))
                        return inst
                    P.op("tensor", fnd, reads=[("h", hp), ("wdT", hc)],
                         writes=[PSK[2 + j] for j in range(NSUB * NCH)])
                for hc in range(HT):
                    hp = hc % 2
                    def fng(t, hc=hc, par=par):
                        for ct in range(CT):
                            inst = t.matmul(psum[0][:, 0:BCH], wgT[:, ct, hc * 128:(hc + 1) * 128],
                                            xn_fm[par][:, ct, :], start=(ct == 0), stop=(ct == CT - 1))
                        return inst
                    P.op("tensor", fng, reads=[("xn_fm", par), ("wgT", hc)], writes=[PSK[0]])
                    def fnu(t, hc=hc, par=par):
                        for ct in range(CT):
                            inst = t.matmul(psum[1][:, 0:BCH], wuT[:, ct, hc * 128:(hc + 1) * 128],
                                            xn_fm[par][:, ct, :], start=(ct == 0), stop=(ct == CT - 1))
                        return inst
                    P.op("tensor", fnu, reads=[("xn_fm", par), ("wuT", hc)], writes=[PSK[1]])
                    P.op("scalar", lambda s, hp=hp: s.activation(p_sb[hp][:], psum[0][:, 0:BCH], AF.Silu, scale=SW),
                         reads=[PSK[0]], writes=[("p", hp)])
                    P.op("vector", lambda v, hp=hp: v.tensor_tensor(h_sb[hp][:], p_sb[hp][:], psum[1][:, 0:BCH], ALU.mult),
                         reads=[("p", hp), PSK[1]], writes=[("h", hp)])
                    if hc == 1 and pend_g[0] is not None:
                        pend_g[0]()
                        pend_g[0] = None
                    if hc >= 1:
                        emit_down(hc - 1)
                    if pend:
                        pend.pop(0)()
                emit_down(HT - 1)
                def g_evacs(b=b, tc=tc, par=par):
                    for tt in range(NSUB):
                        for ic, (co, cl) in enumerate(CCH):
                            bank = 2 + tt * NCH + ic
                            P.op("scalar", lambda s, tt=tt, co=co, cl=cl, bank=bank, par=par:
                                 s.activation(g_sb[par][:, tt, co:co + cl], psum[bank][:, 0:cl],
                                              AF.Silu, scale=SW * SW),
                                 reads=[PSK[bank]], writes=[("g_sb", par, tt)])
                    P.dma("qgps", g_s[b, tc * NSUB:tc * NSUB + NSUB, :, :].rearrange("a p c -> p a c"),
                          g_sb[par][:], reads=[("g_sb", par, tt2) for tt2 in range(NSUB)])
                pend_g[0] = g_evacs

            pend_g = [None]
            for piece in ln_pieces(0):
                piece()
            for i in range(NBC):
                pend = ln_pieces(i + 1) if i + 1 < NBC else []
                emit_ffn(i, pend)
                for piece in pend:
                    piece()
            if pend_g[0] is not None:
                pend_g[0]()
                pend_g[0] = None

        P.barrier()

        with ExitStack() as phC:
            eC = phC.enter_context
            keff = eC(nc.sbuf_tensor("keff", [128, CT, L], BF16))
            veff = eC(nc.sbuf_tensor("veff", [128, LT, C], BF16))
            WuqT = eC(nc.sbuf_tensor("WuqT", [128, QLT, C], BF16))
            WdqT = eC(nc.sbuf_tensor("WdqT", [128, CT, QL], BF16))
            WdkvT = eC(nc.sbuf_tensor("WdkvT", [128, CT, L], BF16))
            xn_b = eC(nc.sbuf_tensor("xn_b", [128, CT, T], BF16))
            ckv_fm = eC(nc.sbuf_tensor("ckv_fm", [128, LT, T], BF16))
            ckv_tm = eC(nc.sbuf_tensor("ckv_tm", [128, TT, L], BF16))
            A_fm = eC(nc.sbuf_tensor("A_fm", [128, QLT, T], BF16))
            q2c = eC(nc.sbuf_tensor("q2c", [128, CT, QCH], BF16))
            qlat = eC(nc.sbuf_tensor("qlat", [128, LT, QCH], BF16))
            ylat = eC(nc.sbuf_tensor("ylat", [128, LT, QCH], BF16))
            exp_sb = [eC(nc.sbuf_tensor("exp%d" % i, [128, QCH], BF16)) for i in range(3)]
            s_sb = eC(nc.sbuf_tensor("s_sb", [128, NKP, C], F32))
            den_sb = eC(nc.sbuf_tensor("den_sb", [1, QCH], F32))
            rec_sb = eC(nc.sbuf_tensor("rec_sb", [128, NKP], F32))
            g_c = [eC(nc.sbuf_tensor("g_c%d" % i, [128, NKP, C], BF16)) for i in range(2)]
            xr_full = eC(nc.sbuf_tensor("xr_full", [128, NKP, C], F32))
            out_st = [eC(nc.sbuf_tensor("out_st%d" % i, [128, C], F32)) for i in range(2)]

            P.dma("qscal", keff[:], keff_s, writes=[("keff",)])
            P.dma("qscal", veff[:], veff_s, writes=[("veff",)])
            P.dma("qscal", WuqT[:], WuqT_s, writes=[("WuqT",)])
            P.dma("qscal", WdqT[:], WdqT_s, writes=[("WdqT",)])
            P.dma("qscal", WdkvT[:], WdkvT_s, writes=[("WdkvT",)])

            def load_xn(bn):
                half = T // 2
                for hh in range(2):
                    P.dma("qsync" if hh == 0 else "qscal", xn_b[:, :, hh * half:(hh + 1) * half],
                          xn_s[bn][:, :, hh * half:(hh + 1) * half].rearrange("a p t -> p a t"),
                          reads=[("xn_b", hh)], writes=[("xn_b", hh)])
            load_xn(0)
            for b in range(B):
                def proj(dst, wT, wkey, MT, dkey):
                    for mc in range(MT):
                        for ic, (to, tl) in enumerate(TCH):
                            k = mc * len(TCH) + ic
                            bank = k % 4
                            def fn(t, mc=mc, to=to, tl=tl, bank=bank, wT=wT):
                                for ct in range(CT):
                                    inst = t.matmul(psum[bank][:, 0:tl], wT[:, ct, mc * 128:(mc + 1) * 128],
                                                    xn_b[:, ct, to:to + tl],
                                                    start=(ct == 0), stop=(ct == CT - 1))
                                return inst
                            xnkeys = [("xn_b", hh) for hh in range(2)
                                      if to < (hh + 1) * (T // 2) and to + tl > hh * (T // 2)]
                            P.op("tensor", fn, reads=xnkeys + [wkey], writes=[PSK[bank]])
                            if k % 2 == 0:
                                P.op("scalar", lambda s, mc=mc, to=to, tl=tl, bank=bank, dst=dst:
                                     s.copy(dst[:, mc, to:to + tl], psum[bank][:, 0:tl]),
                                     reads=[PSK[bank]], writes=[dkey])
                            else:
                                P.op("vector", lambda v, mc=mc, to=to, tl=tl, bank=bank, dst=dst:
                                     v.tensor_copy(dst[:, mc, to:to + tl], psum[bank][:, 0:tl]),
                                     reads=[PSK[bank]], writes=[dkey])
                proj(ckv_fm, WdkvT, ("WdkvT",), LT, ("ckv_fm", b))
                proj(A_fm, WdqT, ("WdqT",), QLT, ("A_fm", b))
                for kt in range(TT):
                    bank = 2 + kt % 2
                    def fn(t, kt=kt, bank=bank):
                        for lc in range(LT):
                            inst = t.transpose(psum_bf(bank)[:, lc * 128:(lc + 1) * 128],
                                               ckv_fm[:, lc, kt * 128:(kt + 1) * 128], ident_b[:])
                        return inst
                    P.op("tensor", fn, reads=[("identb",), ("ckv_fm", b)], writes=[PSK[bank]])
                    P.op("scalar", lambda s, kt=kt, bank=bank: s.copy(ckv_tm[:, kt, 0:L], psum_bf(bank)[:, 0:L]),
                         reads=[PSK[bank]], writes=[("ckv_tm", b)])
                if b + 1 < B:
                    load_xn(b + 1)

                qlat_hoisted = [False]
                def emit_load_xr(qcn):
                    P.dma("qscal", xr_full[:], x_d[b, qcn * QCH:(qcn + 1) * QCH, :].rearrange("(j p) c -> p j c", p=128),
                          reads=[("xr",)], writes=[("xr",)])
                def emit_combine(qcn):
                    for j in range(NKP):
                        xp = j % 2
                        tq = qcn * QCH + j * 128
                        P.op("vector", lambda v, j=j, xp=xp: v.tensor_tensor(out_st[xp][:], s_sb[:, j, :], xr_full[:, j, :], ALU.add),
                             reads=[("s_sb", j), ("xr",)], writes=[("out_st", xp)])
                        P.dma("qgps", out_d[b, tq:tq + 128, :], out_st[xp][:],
                              reads=[("out_st", xp)])
                for qc in range(NQC):
                    q0 = qc * QCH
                    gpar = qc % 2
                    P.dma("qscal", g_c[gpar][:], g_s[b, qc * NKP:(qc + 1) * NKP, :, :].rearrange("a p c -> p a c"),
                          reads=[("g_c", gpar)], writes=[("g_c", gpar)])
                    def emit_q2c(qcn):
                        for cch in range(CT):
                            bank = 4 + cch % 2
                            def fn(t, cch=cch, bank=bank, qcn=qcn):
                                for qlt in range(QLT):
                                    inst = t.matmul(psum[bank][:, 0:QCH], WuqT[:, qlt, cch * 128:(cch + 1) * 128],
                                                    A_fm[:, qlt, qcn * QCH:qcn * QCH + QCH],
                                                    start=(qlt == 0), stop=(qlt == QLT - 1))
                                return inst
                            P.op("tensor", fn, reads=[("A_fm", b), ("WuqT",)], writes=[PSK[bank]])
                            if cch % 2 == 0:
                                P.op("scalar", lambda s, cch=cch, bank=bank: s.copy(q2c[:, cch, :], psum[bank][:, 0:QCH]),
                                     reads=[PSK[bank]], writes=[("q2c", cch)])
                            else:
                                P.op("vector", lambda v, cch=cch, bank=bank: v.tensor_copy(q2c[:, cch, :], psum[bank][:, 0:QCH]),
                                     reads=[PSK[bank]], writes=[("q2c", cch)])
                    if qc == 0:
                        emit_q2c(0)

                    QB = [4, 5, 7, 0]
                    ktmax = (qc + 1) * NKP

                    def emit_qlat(h):
                        hrow = (h % HPT) * HS
                        htile = h // HPT
                        def fq(t, hrow=hrow, htile=htile):
                            for lc in range(LT - 1):
                                inst = t.matmul(psum[QB[lc]][:, 0:QCH],
                                                keff[hrow:hrow + HS, htile, lc * 128:(lc + 1) * 128],
                                                q2c[hrow:hrow + HS, htile, :], start=True, stop=True)
                            return inst
                        P.op("tensor", fq, reads=[("keff",), ("q2c", htile)],
                             writes=[PSK[QB[lc2]] for lc2 in range(LT - 1)])
                        P.op("tensor", lambda t, hrow=hrow, htile=htile, lc=LT - 1:
                             t.matmul(psum[QB[lc]][:, 0:QCH],
                                      keff[hrow:hrow + HS, htile, lc * 128:(lc + 1) * 128],
                                      q2c[hrow:hrow + HS, htile, :], start=True, stop=True),
                             reads=[("keff",), ("q2c", htile)], writes=[PSK[QB[LT - 1]]])
                        for lc in range(LT):
                            if lc % 2 == 0:
                                P.op("scalar", lambda s, lc=lc: s.copy(qlat[:, lc, :], psum[QB[lc]][:, 0:QCH]),
                                     reads=[PSK[QB[lc]]], writes=[("qlat",)])
                            else:
                                P.op("vector", lambda v, lc=lc: v.tensor_copy(qlat[:, lc, :], psum[QB[lc]][:, 0:QCH]),
                                     reads=[PSK[QB[lc]]], writes=[("qlatv",)])

                    def qoff(kt):
                        mi = kt - qc * NKP
                        return mi * 128 if mi > 0 else 0

                    def emit_scores(kt):
                        sb = 4 + kt % 2
                        ep = kt % 3
                        qo = qoff(kt)
                        diag = kt >= qc * NKP
                        def fs(t, kt=kt, sb=sb, qo=qo, b=b, diag=diag):
                            for lc in range(LT):
                                inst = t.matmul(psum[sb][:, qo:QCH], ckv_fm[:, lc, kt * 128:(kt + 1) * 128],
                                                qlat[:, lc, qo:QCH], start=(lc == 0), stop=(lc == LT - 1 and not diag))
                            if diag:
                                inst = t.matmul(psum[sb][:, qo:QCH], ident_b[:], negtri[:, 0:QCH - qo],
                                                start=False, stop=True)
                            return inst
                        P.op("tensor", fs, reads=[("ckv_fm", b), ("qlat",), ("qlatv",), ("negtri",), ("identb",)],
                             writes=[PSK[sb]])
                        P.op("scalar", lambda s, sb=sb, ep=ep, qo=qo: s.activation(
                                exp_sb[ep][:, qo:QCH], psum[sb][:, qo:QCH], AF.Exp, scale=0.125),
                             reads=[PSK[sb]], writes=[("exp", ep)])

                    def emit_ylat(kt, split=False, evac=False):
                        ep = kt % 3
                        qo = qoff(kt)
                        last = (kt == ktmax - 1)
                        def emit_den():
                            P.op("tensor", lambda t, ep=ep, qo=qo, first=(kt == 0), last=last:
                                 t.matmul(psum[6][:, qo:QCH], ones_m[:], exp_sb[ep][:, qo:QCH],
                                          start=first, stop=last),
                                 reads=[("exp", ep), ("ones",)], writes=[PSK[6]], inc=last)
                        if not split:
                            def fy(t, kt=kt, ep=ep, qo=qo, b=b, first=(kt == 0), last=last):
                                for lc in range(LT):
                                    inst = t.matmul(psum[lc][:, qo:QCH], ckv_tm[:, kt, lc * 128:(lc + 1) * 128],
                                                    exp_sb[ep][:, qo:QCH], start=first, stop=last)
                                return inst
                            P.op("tensor", fy, reads=[("ckv_tm", b), ("exp", ep)],
                                 writes=[PSK[lc] for lc in range(LT)], inc=False)
                            emit_den()
                        else:
                            emit_den()
                            for lc in range(LT):
                                P.op("tensor", lambda t, kt=kt, ep=ep, qo=qo, b=b, lc=lc,
                                     first=(kt == 0), last=last:
                                     t.matmul(psum[lc][:, qo:QCH], ckv_tm[:, kt, lc * 128:(lc + 1) * 128],
                                              exp_sb[ep][:, qo:QCH], start=first, stop=last),
                                     reads=[("ckv_tm", b), ("exp", ep)], writes=[PSK[lc]])
                                if evac:
                                    if lc % 2 == 0:
                                        P.op("scalar", lambda s, lc=lc: s.copy(ylat[:, lc, :], psum[lc][:, 0:QCH]),
                                             reads=[PSK[lc]], writes=[("ylat",)])
                                    else:
                                        P.op("vector", lambda v, lc=lc: v.tensor_copy(ylat[:, lc, :], psum[lc][:, 0:QCH]),
                                             reads=[PSK[lc]], writes=[("ylatv",)])
                                    if lc == 0:
                                        P.op("scalar", lambda s: s.copy(den_sb[:], psum[6][0:1, 0:QCH]),
                                             reads=[PSK[6]], writes=[("den",)])

                    def emit_y(h):
                        for j in list(range(1, NKP)) + [0]:
                            def fyo(t, j=j, h=h):
                                for lc in range(LT):
                                    inst = t.matmul(psum[j][:, 0:HS], ylat[:, lc, j * 128:(j + 1) * 128],
                                                    veff[:, lc, h * HS:(h + 1) * HS],
                                                    start=(lc == 0), stop=(lc == LT - 1))
                                return inst
                            P.op("tensor", fyo, reads=[("ylat",), ("ylatv",), ("veff",)], writes=[PSK[j]])
                            P.op("vector", lambda v, j=j, h=h, gpar=gpar: v.scalar_tensor_tensor(
                                    s_sb[:, j, h * HS:(h + 1) * HS], psum[j][:, 0:HS],
                                    rec_sb[:, j:j + 1], g_c[gpar][:, j, h * HS:(h + 1) * HS],
                                    ALU.mult, ALU.mult),
                                 reads=[PSK[j], ("rec",), ("g_c", gpar)], writes=[("s_sb", j)])

                    def emit_fdt_recip():
                        def fdt(t):
                            for j in range(NKP):
                                inst = t.matmul(psum[6][:, j:j + 1], den_sb[0:1, j * 128:(j + 1) * 128],
                                                ones_f[0:1, 0:1], start=True, stop=True)
                            return inst
                        P.op("tensor", fdt, reads=[("den",), ("ones",)], writes=[PSK[6]])
                        P.op("vector", lambda v: v.reciprocal(rec_sb[:, 0:NKP], psum[6][:, 0:NKP]),
                             reads=[PSK[6]], writes=[("rec",)])
                    if qc == 0:
                        emit_load_xr(0)
                    prev = None
                    for h in range(NH):
                        if h > 0 or not qlat_hoisted[0]:
                            emit_qlat(h)
                        if h == 0:
                            qlat_hoisted[0] = False
                        if prev is not None:
                            emit_fdt_recip()
                            emit_y(prev)
                        for kt in range(ktmax):
                            emit_scores(kt)
                            if kt >= 1:
                                emit_ylat(kt - 1)
                        emit_ylat(ktmax - 1, split=True, evac=True)
                        prev = h
                    emit_fdt_recip()
                    if qc + 1 < NQC:
                        emit_q2c(qc + 1)
                        emit_qlat(0)
                        qlat_hoisted[0] = True
                    emit_y(prev)
                    emit_combine(qc)
                    if qc + 1 < NQC:
                        emit_load_xr(qc + 1)

            P.barrier()
            block = ent(nc.Block())
            P.emit(block, sems)
    return nc


def kernel(L0_attn_x, L0_attn_W_dq, L0_attn_W_uq, L0_attn_W_dkv, L0_attn_W_uk,
           L0_attn_W_uv, L0_attn_W_o, L0_ffn_w_gate, L0_ffn_w_up, L0_ffn_w_down):
    n = 8
    x = np.ascontiguousarray(np.asarray(L0_attn_x, dtype=np.float32))
    shards = np.split(x, n, axis=0)
    weights = {
        "W_dq": np.ascontiguousarray(np.asarray(L0_attn_W_dq, dtype=np.float32)),
        "W_uq": np.ascontiguousarray(np.asarray(L0_attn_W_uq, dtype=np.float32)),
        "W_dkv": np.ascontiguousarray(np.asarray(L0_attn_W_dkv, dtype=np.float32)),
        "W_uk": np.ascontiguousarray(np.asarray(L0_attn_W_uk, dtype=np.float32)),
        "W_uv": np.ascontiguousarray(np.asarray(L0_attn_W_uv, dtype=np.float32)),
        "W_o": np.ascontiguousarray(np.asarray(L0_attn_W_o, dtype=np.float32)),
        "w_gate": np.ascontiguousarray(np.asarray(L0_ffn_w_gate, dtype=np.int32)),
        "w_up": np.ascontiguousarray(np.asarray(L0_ffn_w_up, dtype=np.int32)),
        "w_down": np.ascontiguousarray(np.asarray(L0_ffn_w_down, dtype=np.int32)),
    }
    nc = bass.Bass("TRN2", target_bir_lowering=False)
    build(nc, CFG_FULL)
    in_maps = [dict(weights, x=shards[i]) for i in range(n)]
    res = run_bass_kernel_spmd(nc, in_maps, core_ids=list(range(n)))
    return np.concatenate([np.asarray(r["out"]) for r in res.results], axis=0)


# version 36
# speedup vs baseline: 1.0291x; 1.0291x over previous
import numpy as np
import concourse.bass as bass
import concourse.mybir as mybir
from concourse.bass_utils import run_bass_kernel_spmd
from contextlib import ExitStack

F32 = mybir.dt.float32
BF16 = mybir.dt.bfloat16
I32 = mybir.dt.int32
AF = mybir.ActivationFunctionType
ALU = mybir.AluOpType
AX = mybir.AxisListType

QCH = 512
BCH = 256

CFG_FULL = dict(B=4, T=2048, C=1024, NH=16, HS=64, L=512, QL=512, H=3072)


class Prog:

    STREAMS = ("tensor", "vector", "scalar", "gpsimd", "sync")
    QSTREAM = {"qsync": "sync", "qscal": "scalar", "qgps": "gpsimd"}
    NSLOT = 8

    def __init__(self):
        self.items = {s: [] for s in self.STREAMS}
        self.cnt = {s: 0 for s in self.STREAMS}
        self.qslot = {q: 0 for q in self.QSTREAM}
        for q in self.QSTREAM:
            for i in range(self.NSLOT):
                self.cnt[q + str(i)] = 0
        self.res = {}
        self.satisfied = {s: {} for s in self.STREAMS}
        self.max_waited = {}
        self.n_ops = 0

    def sem_names(self):
        names = list(self.STREAMS)
        for q in self.QSTREAM:
            names += [q + str(i) for i in range(self.NSLOT)]
        return names

    def _wait(self, stream, tok):
        sem, val = tok
        if sem == stream == "tensor":
            return
        if self.satisfied[stream].get(sem, 0) >= val:
            return
        self.items[stream].append(("wait", sem, val))
        self.satisfied[stream][sem] = val
        v = self.max_waited.get(sem, 0)
        if val > v:
            self.max_waited[sem] = val

    def _deps(self, stream, reads, writes):
        toks = []
        for r in reads:
            e = self.res.get(r)
            if e and e[0] is not None:
                toks.append(e[0])
        for w in writes:
            e = self.res.get(w)
            if e:
                if e[0] is not None:
                    toks.append(e[0])
                toks.extend(e[1])
        for t in sorted(set(toks)):
            self._wait(stream, t)

    def _mark(self, tok, reads, writes):
        for w in writes:
            self.res[w] = [tok, []]
        for r in reads:
            e = self.res.setdefault(r, [None, []])
            e[1].append(tok)

    def op(self, stream, fn, reads=(), writes=(), inc=True):
        self._deps(stream, reads, writes)
        if inc:
            self.cnt[stream] += 1
            tok = (stream, self.cnt[stream])
            self.items[stream].append(("op", fn, stream, 1))
        else:
            tok = (stream, self.cnt[stream] + 1)
            self.items[stream].append(("op", fn, None, 0))
        self._mark(tok, reads, writes)
        self.n_ops += 1

    def dma(self, queue, out_ap, in_ap, reads=(), writes=()):
        stream = self.QSTREAM[queue]
        slot = self.qslot[queue]
        self.qslot[queue] = (slot + 1) % self.NSLOT
        sem = queue + str(slot)
        self._wait(stream, (sem, self.cnt[sem]))
        self._deps(stream, reads, writes)
        self.cnt[sem] += 16
        tok = (sem, self.cnt[sem])
        def fn(eng, o=out_ap, i=in_ap):
            return eng.dma_start(out=o, in_=i)
        self.items[stream].append(("op", fn, sem, 16))
        self._mark(tok, reads, writes)
        self.n_ops += 1

    def barrier(self):
        snap = [(sem, val) for sem, val in self.cnt.items() if val > 0]
        for s in self.STREAMS:
            for sem, val in snap:
                self._wait(s, (sem, val))

    def finalize_check(self):
        for sem, val in self.max_waited.items():
            assert val <= self.cnt[sem], (
                f"wait on {sem} >= {val} but final count is {self.cnt[sem]} (hang)")

    def emit(self, block, sems):
        self.finalize_check()
        def mk(stream):
            def body(eng):
                for it in self.items[stream]:
                    if it[0] == "wait":
                        eng.wait_ge(sems[it[1]], it[2])
                    else:
                        inst = it[1](eng)
                        if it[2] is not None:
                            inst.then_inc(sems[it[2]], it[3])
            return body
        for stream in self.STREAMS:
            getattr(block, stream)(mk(stream))


def build(nc, cfg):
    B, T, C, NH, HS = cfg["B"], cfg["T"], cfg["C"], cfg["NH"], cfg["HS"]
    L, QL, H = cfg["L"], cfg["QL"], cfg["H"]
    CT, LT, QLT, HT, TT = C // 128, L // 128, QL // 128, H // 128, T // 128
    NQC = T // QCH
    NKP = QCH // 128
    NBC = B * T // BCH
    SW = 1.0 / 127.0
    def chunks(total, size=512):
        return [(o, min(size, total - o)) for o in range(0, total, size)]
    CCH = chunks(C)
    TCH = chunks(T)
    NCH = len(CCH)
    HPT = 128 // HS

    x_d = nc.dram_tensor("x", [B, T, C], F32, kind="ExternalInput").ap()
    Wdq_d = nc.dram_tensor("W_dq", [QL, C], F32, kind="ExternalInput").ap()
    Wuq_d = nc.dram_tensor("W_uq", [C, QL], F32, kind="ExternalInput").ap()
    Wdkv_d = nc.dram_tensor("W_dkv", [L, C], F32, kind="ExternalInput").ap()
    Wuk_d = nc.dram_tensor("W_uk", [C, L], F32, kind="ExternalInput").ap()
    Wuv_d = nc.dram_tensor("W_uv", [C, L], F32, kind="ExternalInput").ap()
    Wo_d = nc.dram_tensor("W_o", [C, C], F32, kind="ExternalInput").ap()
    wg_d = nc.dram_tensor("w_gate", [H, C], I32, kind="ExternalInput").ap()
    wu_d = nc.dram_tensor("w_up", [H, C], I32, kind="ExternalInput").ap()
    wd_d = nc.dram_tensor("w_down", [C, H], I32, kind="ExternalInput").ap()
    out_d = nc.dram_tensor("out", [B, T, C], F32, kind="ExternalOutput").ap()
    xn_s = nc.dram_tensor("xn_s", [B, CT, 128, T], BF16, kind="Internal").ap()
    g_s = nc.dram_tensor("g_s", [B, TT, 128, C], BF16, kind="Internal").ap()
    keff_s = nc.dram_tensor("keff_s", [128, CT, L], BF16, kind="Internal").ap()
    veff_s = nc.dram_tensor("veff_s", [128, LT, C], BF16, kind="Internal").ap()
    WuqT_s = nc.dram_tensor("WuqT_s", [128, QLT, C], BF16, kind="Internal").ap()
    WdqT_s = nc.dram_tensor("WdqT_s", [128, CT, QL], BF16, kind="Internal").ap()
    WdkvT_s = nc.dram_tensor("WdkvT_s", [128, CT, L], BF16, kind="Internal").ap()

    P = Prog()

    with ExitStack() as top:
        ent = top.enter_context
        sems = {}
        for name in P.sem_names():
            sems[name] = ent(nc.semaphore("sem_" + name))

        ident_b = ent(nc.sbuf_tensor("ident_b", [128, 128], BF16))
        ident_f = ent(nc.sbuf_tensor("ident_f", [128, 128], F32))
        negtri = ent(nc.sbuf_tensor("negtri", [128, QCH], BF16))
        ones_b = ent(nc.sbuf_tensor("ones_b", [128, 4], BF16))
        ones_f = ent(nc.sbuf_tensor("ones_f", [128, 4], F32))
        ones_m = ent(nc.sbuf_tensor("ones_m", [128, 128], BF16))
        iota_f = ent(nc.sbuf_tensor("iota_f", [128, QCH], F32))
        stat = ent(nc.sbuf_tensor("stat", [128, 16], F32))

        psum = [ent(nc.psum_tensor("ps%d" % i, [128, 512], F32)) for i in range(8)]
        PSK = ["PS%d" % i for i in range(8)]

        def psum_bf(i):
            return psum[i][:].bitcast(BF16)

        def mk_iota(dst, n, base):
            def fn(eng, d=dst, nn=n, bb=base):
                return eng.iota(d, [[1, nn]], base=bb, channel_multiplier=-1,
                                allow_small_or_imprecise_dtypes=True)
            return fn
        P.op("gpsimd", mk_iota(iota_f[:, 0:128], 128, 0), writes=[("io",)])
        P.op("vector", lambda v: v.tensor_single_scalar(ident_b[:], iota_f[:, 0:128], 0.0, ALU.is_equal),
             reads=[("io",)], writes=[("identb",)])
        P.op("vector", lambda v: v.tensor_single_scalar(ident_f[:], iota_f[:, 0:128], 0.0, ALU.is_equal),
             reads=[("io",)], writes=[("identf",)])
        P.op("gpsimd", mk_iota(iota_f[:], QCH, 0), writes=[("io",)])
        P.op("vector", lambda v: v.tensor_single_scalar(negtri[:], iota_f[:], 0.0, ALU.is_lt),
             reads=[("io",)], writes=[("negtri",)])
        P.op("vector", lambda v: v.tensor_scalar(negtri[:], negtri[:], -100000.0, None, ALU.mult),
             reads=[("negtri",)], writes=[("negtri",)])
        P.op("vector", lambda v: v.memset(ones_b[:], 1.0), writes=[("ones",)])
        P.op("vector", lambda v: v.memset(ones_f[:], 1.0), writes=[("ones",)])
        P.op("vector", lambda v: v.memset(ones_m[:], 1.0), writes=[("ones",)])

        with ExitStack() as ph0:
            e0 = ph0.enter_context
            Wuq_f = e0(nc.sbuf_tensor("Wuq_f", [128, CT, QL], F32))
            Wuk_f = e0(nc.sbuf_tensor("Wuk_f", [128, CT, L], F32))
            Wdq_f = e0(nc.sbuf_tensor("Wdq_f", [128, QLT, C], F32))
            Wdkv_f = e0(nc.sbuf_tensor("Wdkv_f", [128, LT, C], F32))
            Wuv_f = e0(nc.sbuf_tensor("Wuv_f", [128, CT, L], F32))
            Wo_f = e0(nc.sbuf_tensor("Wo_f", [128, CT, C], F32))
            WoT_f = e0(nc.sbuf_tensor("WoT_f", [128, CT, C], F32))
            tmp1 = e0(nc.sbuf_tensor("tmp1", [128, QLT, L], F32))
            stage_b = e0(nc.sbuf_tensor("stage_b", [128, 16, 1024], BF16))
            wevac = e0(nc.sbuf_tensor("wevac", [128, 1024], BF16))
            keff_sb = e0(nc.sbuf_tensor("keff_sb", [128, CT, L], BF16))
            veff_sb = e0(nc.sbuf_tensor("veff_sb", [128, LT, C], BF16))

            def wload(dst, src_view, key, ntile):
                hlf = max(1, ntile // 2)
                for hh, (a0, a1) in enumerate([(0, hlf), (hlf, ntile)]):
                    if a1 > a0:
                        P.dma("qsync", dst[:, a0:a1, :], src_view[:, a0:a1, :], writes=[key, (key[0], hh)])
            wload(Wuq_f, Wuq_d.rearrange("(a p) q -> p a q", p=128), ("Wuq_f",), CT)
            wload(Wuk_f, Wuk_d.rearrange("(a p) l -> p a l", p=128), ("Wuk_f",), CT)
            wload(Wdq_f, Wdq_d.rearrange("(a p) c -> p a c", p=128), ("Wdq_f",), QLT)
            wload(Wdkv_f, Wdkv_d.rearrange("(a p) c -> p a c", p=128), ("Wdkv_f",), LT)
            wload(Wuv_f, Wuv_d.rearrange("(a p) l -> p a l", p=128), ("Wuv_f",), CT)
            wload(Wo_f, Wo_d.rearrange("(a p) c -> p a c", p=128), ("Wo_f",), CT)

            for qlc in range(QLT):
                bank = qlc % 2
                def fn(t, qlc=qlc, bank=bank):
                    for ct in range(CT):
                        inst = t.matmul(psum[bank][:, 0:L], Wuq_f[:, ct, qlc * 128:(qlc + 1) * 128],
                                        Wuk_f[:, ct, 0:L], start=(ct == 0), stop=(ct == CT - 1))
                    return inst
                P.op("tensor", fn, reads=[("Wuq_f",), ("Wuk_f",)], writes=[PSK[bank]])
                P.op("scalar", lambda s, qlc=qlc, bank=bank: s.copy(tmp1[:, qlc, :], psum[bank][:, 0:L]),
                     reads=[PSK[bank]], writes=[("tmp1", qlc)])
            for cch in range(CT):
                bank = cch % 2
                def fn(t, cch=cch, bank=bank):
                    for qlt in range(QLT):
                        inst = t.matmul(psum[bank][:, 0:L], Wdq_f[:, qlt, cch * 128:(cch + 1) * 128],
                                        tmp1[:, qlt, :], start=(qlt == 0), stop=(qlt == QLT - 1))
                    return inst
                P.op("tensor", fn, reads=[("Wdq_f",)] + [("tmp1", q) for q in range(QLT)], writes=[PSK[bank]])
                P.op("scalar", lambda s, cch=cch, bank=bank: s.copy(keff_sb[:, cch, :], psum[bank][:, 0:L]),
                     reads=[PSK[bank]], writes=[("keff_sb",)])
            P.dma("qgps", keff_s, keff_sb[:], reads=[("keff_sb",)])

            def emit_T(src_f32, src_key, rows_t, cols_t, dst_dram, tag, row0=0):
                for rt in range(rows_t):
                    hh = 0 if rt < max(1, rows_t // 2) else 1
                    P.op("scalar", lambda s, rt=rt: s.copy(stage_b[:, row0 + rt, 0:cols_t * 128], src_f32[:, rt, :]),
                         reads=[(src_key[0], hh)], writes=[(tag + "stg", rt)])
                for cc in range(cols_t):
                    bank = cc % 2
                    def fn(t, cc=cc, bank=bank):
                        for rt in range(rows_t):
                            inst = t.transpose(psum_bf(bank)[:, rt * 128:(rt + 1) * 128],
                                               stage_b[:, row0 + rt, cc * 128:(cc + 1) * 128], ident_b[:])
                        return inst
                    P.op("tensor", fn, reads=[("identb",)] + [(tag + "stg", rt) for rt in range(rows_t)],
                         writes=[PSK[bank]])
                    P.op("scalar", lambda s, cc=cc, bank=bank: s.copy(wevac[:, 0:rows_t * 128],
                                                                      psum_bf(bank)[:, 0:rows_t * 128]),
                         reads=[PSK[bank]], writes=[("wevac",)])
                    P.dma("qgps", dst_dram[:, cc, :], wevac[:, 0:rows_t * 128], reads=[("wevac",)])

            emit_T(Wuq_f, ("Wuq_f",), CT, QLT, WuqT_s, "wuq", row0=0)
            emit_T(Wdq_f, ("Wdq_f",), QLT, CT, WdqT_s, "wdq", row0=CT)
            emit_T(Wdkv_f, ("Wdkv_f",), LT, CT, WdkvT_s, "wdkv", row0=CT + QLT)

            for cc in range(CT):
                for half in range((CT + 3) // 4):
                    nrt = min(4, CT - half * 4)
                    bank = (cc * 2 + half) % 2 + 2
                    def fn(t, cc=cc, half=half, bank=bank, nrt=nrt):
                        for rt in range(nrt):
                            rtg = half * 4 + rt
                            inst = t.transpose(psum[bank][:, rt * 128:(rt + 1) * 128],
                                               Wo_f[:, rtg, cc * 128:(cc + 1) * 128], ident_f[:])
                        return inst
                    wokeys = sorted({("Wo_f", 0 if (half * 4 + rt) < max(1, CT // 2) else 1)
                                     for rt in range(nrt)})
                    P.op("tensor", fn, reads=[("identf",)] + wokeys, writes=[PSK[bank]])
                    P.op("scalar", lambda s, cc=cc, half=half, bank=bank, nrt=nrt:
                         s.copy(WoT_f[:, cc, half * 512:half * 512 + nrt * 128], psum[bank][:, 0:nrt * 128]),
                         reads=[PSK[bank]], writes=[("WoT", cc)])
            for lc in range(LT):
                for ic, (co, cl) in enumerate(CCH):
                    bank = (lc + ic) % 2 + 4
                    def fn(t, lc=lc, co=co, cl=cl, bank=bank):
                        for ct in range(CT):
                            inst = t.matmul(psum[bank][:, 0:cl], Wuv_f[:, ct, lc * 128:(lc + 1) * 128],
                                            WoT_f[:, ct, co:co + cl],
                                            start=(ct == 0), stop=(ct == CT - 1))
                        return inst
                    P.op("tensor", fn, reads=[("Wuv_f",)] + [("WoT", c2) for c2 in range(CT)], writes=[PSK[bank]])
                    P.op("scalar", lambda s, lc=lc, co=co, cl=cl, bank=bank:
                         s.copy(veff_sb[:, lc, co:co + cl], psum[bank][:, 0:cl]),
                         reads=[PSK[bank]], writes=[("veff_sb",)])
            P.dma("qgps", veff_s, veff_sb[:], reads=[("veff_sb",)])

        P.barrier()

        with ExitStack() as phB:
            eB = phB.enter_context
            wgT = eB(nc.sbuf_tensor("wgT", [128, CT, H], BF16))
            wuT = eB(nc.sbuf_tensor("wuT", [128, CT, H], BF16))
            wdT = eB(nc.sbuf_tensor("wdT", [128, HT, C], BF16))
            wstg = eB(nc.sbuf_tensor("wstg", [128, 2, H], BF16))
            sq_dummy = eB(nc.sbuf_tensor("sq_dummy", [128, 1024], F32))
            x_c = [eB(nc.sbuf_tensor("x_c%d" % i, [128, BCH // 128, C], F32)) for i in range(2)]
            xn_tm = eB(nc.sbuf_tensor("xn_tm", [128, BCH // 128, C], BF16))
            xn_fm = [eB(nc.sbuf_tensor("xn_fm%d" % i, [128, CT, BCH], BF16)) for i in range(2)]
            p_sb = [eB(nc.sbuf_tensor("p_sb%d" % i, [128, BCH], BF16)) for i in range(2)]
            h_sb = [eB(nc.sbuf_tensor("h_sb%d" % i, [128, BCH], BF16)) for i in range(2)]
            g_sb = [eB(nc.sbuf_tensor("g_sb%d" % i, [128, BCH // 128, C], BF16)) for i in range(2)]

            def load_ffn(dst, src_d, tiles, tag):
                ctiles = dst.shape[1]
                for tile in range(tiles):
                    sp = tile % 2
                    sbuf = wstg[:, sp, 0:ctiles * 128]
                    P.dma("qgps", sbuf, src_d[tile * 128:(tile + 1) * 128, :],
                          reads=[("wstg", sp)], writes=[("wstg", sp)])
                    ngrp = (ctiles + 7) // 8
                    for grp in range(ngrp):
                        lo, hi = grp * 8, min(grp * 8 + 8, ctiles)
                        bank = (tile * ngrp + grp) % 2
                        def fn(t, lo=lo, hi=hi, bank=bank, sbuf=sbuf):
                            for k in range(hi - lo):
                                inst = t.transpose(psum_bf(bank)[:, k * 128:(k + 1) * 128],
                                                   sbuf[:, (lo + k) * 128:(lo + k + 1) * 128], ident_b[:])
                            return inst
                        P.op("tensor", fn, reads=[("identb",), ("wstg", sp)], writes=[PSK[bank]])
                        def ev(s, tile=tile, lo=lo, hi=hi, bank=bank, dst=dst):
                            return s.copy(dst[:, lo:hi, tile * 128:(tile + 1) * 128],
                                          psum_bf(bank)[:, 0:(hi - lo) * 128].rearrange("p (a c) -> p a c", c=128))
                        P.op("scalar", ev, reads=[PSK[bank]], writes=[(tag, tile)])

            def load_ffn_pair(dst, src_d, tag, tile0, ntile):
                ctiles = dst.shape[1]
                ncols = ctiles * 128
                sp = LF_CNT[0] % 2
                LF_CNT[0] += 1
                sbuf = wstg[:, sp, 0:ntile * ncols]
                def regions(c0, c1):
                    return [("wstg", sp, r) for r in range(c0 // C, (c1 + C - 1) // C)]
                for ti in range(ntile):
                    P.dma("qgps", sbuf[:, ti * ncols:(ti + 1) * ncols],
                          src_d[(tile0 + ti) * 128:(tile0 + ti + 1) * 128, :],
                          reads=regions(ti * ncols, (ti + 1) * ncols),
                          writes=regions(ti * ncols, (ti + 1) * ncols))
                ngrp = (ctiles + 7) // 8
                for ti in range(ntile):
                    tile = tile0 + ti
                    for grp in range(ngrp):
                        lo, hi = grp * 8, min(grp * 8 + 8, ctiles)
                        bank = LF_CNT[0] % 2
                        def fn(t, lo=lo, hi=hi, bank=bank, sbuf=sbuf, ti=ti):
                            for k in range(hi - lo):
                                inst = t.transpose(psum_bf(bank)[:, k * 128:(k + 1) * 128],
                                                   sbuf[:, ti * ncols + (lo + k) * 128: ti * ncols + (lo + k + 1) * 128],
                                                   ident_b[:])
                            return inst
                        P.op("tensor", fn,
                             reads=[("identb",)] + regions(ti * ncols + lo * 128, ti * ncols + hi * 128),
                             writes=[PSK[bank]])
                        def ev(s, tile=tile, lo=lo, hi=hi, bank=bank, dst=dst):
                            return s.copy(dst[:, lo:hi, tile * 128:(tile + 1) * 128],
                                          psum_bf(bank)[:, 0:(hi - lo) * 128].rearrange("p (a c) -> p a c", c=128))
                        P.op("scalar", ev, reads=[PSK[bank]], writes=[(tag, tile)])

            LF_CNT = [0]
            wd_next = [0]
            WG_NT = max(1, min(3, H // C))
            for tp in range(0, HT, WG_NT):
                load_ffn_pair(wgT, wg_d, "wgT", tp, min(WG_NT, HT - tp))
                load_ffn_pair(wuT, wu_d, "wuT", tp, min(WG_NT, HT - tp))
                if wd_next[0] < CT:
                    load_ffn_pair(wdT, wd_d, "wdT", wd_next[0], 1)
                    wd_next[0] += 1
            while wd_next[0] < CT:
                load_ffn_pair(wdT, wd_d, "wdT", wd_next[0], 1)
                wd_next[0] += 1

            NSUB = BCH // 128

            def ln_pieces(i):
                b, tc = i // (T // BCH), i % (T // BCH)
                t0 = tc * BCH
                par = i % 2
                xk = ("x_c", par)
                pieces = []
                def load():
                    P.dma("qsync", x_c[par][:], x_d[b, t0:t0 + BCH, :].rearrange("(s p) c -> p s c", p=128),
                          writes=[xk])
                pieces.append(load)
                def stats_v(s_):
                    P.op("vector", lambda v, par=par, s_=s_: v.reduce_sum(stat[:, s_:s_ + 1], x_c[par][:, s_, :], axis=AX.X),
                         reads=[xk], writes=[("stat",)])
                def stats_s(s_):
                    P.op("scalar", lambda s, par=par, s_=s_: s.activation(sq_dummy[:, 0:C], x_c[par][:, s_, :], AF.Square,
                                                                          accum_out=stat[:, 2 + s_:3 + s_]),
                         reads=[xk], writes=[("stat",), ("sqd",)])
                for s_ in range(NSUB):
                    pieces.append(lambda s_=s_: stats_v(s_))
                    pieces.append(lambda s_=s_: stats_s(s_))
                def meanvar():
                    P.op("vector", lambda v: v.tensor_scalar(stat[:, 4:6], stat[:, 0:2], 1.0 / C, None, ALU.mult),
                         reads=[("stat",)], writes=[("stat",)])
                    P.op("vector", lambda v: v.tensor_tensor(stat[:, 6:8], stat[:, 4:6], stat[:, 4:6], ALU.mult),
                         reads=[("stat",)], writes=[("stat",)])
                    P.op("vector", lambda v: v.scalar_tensor_tensor(stat[:, 8:10], stat[:, 2:4], 1.0 / C, stat[:, 6:8],
                                                                    ALU.mult, ALU.subtract),
                         reads=[("stat",)], writes=[("stat",)])
                    P.op("vector", lambda v: v.tensor_scalar(stat[:, 8:10], stat[:, 8:10], 1e-5, None, ALU.add),
                         reads=[("stat",)], writes=[("stat",)])
                    P.op("scalar", lambda s: s.activation(stat[:, 10:12], stat[:, 8:10], AF.Sqrt),
                         reads=[("stat",)], writes=[("stat",)])
                    P.op("vector", lambda v: v.reciprocal(stat[:, 12:14], stat[:, 10:12]),
                         reads=[("stat",)], writes=[("stat",)])
                pieces.append(meanvar)
                def xn_one(s_, hh):
                    P.op("vector", lambda v, par=par, s_=s_, hh=hh: v.tensor_scalar(
                            xn_tm[:, s_, hh * (C // 2):(hh + 1) * (C // 2)],
                            x_c[par][:, s_, hh * (C // 2):(hh + 1) * (C // 2)],
                            stat[:, 4 + s_:5 + s_], stat[:, 12 + s_:13 + s_],
                            ALU.subtract, ALU.mult),
                         reads=[xk, ("stat",)], writes=[("xn_tm", s_)])
                for s_ in range(NSUB):
                    for hh in range(2):
                        pieces.append(lambda s_=s_, hh=hh: xn_one(s_, hh))
                def tr_one(s_):
                    bank = 6 + (s_ % 2)
                    def fn(t, s_=s_, bank=bank):
                        for ct in range(CT):
                            inst = t.transpose(psum_bf(bank)[:, ct * 128:(ct + 1) * 128],
                                               xn_tm[:, s_, ct * 128:(ct + 1) * 128], ident_b[:])
                        return inst
                    P.op("tensor", fn, reads=[("identb",), ("xn_tm", s_)], writes=[PSK[bank]])
                    def ev(v, s_=s_, bank=bank, par=par):
                        return v.tensor_copy(xn_fm[par][:, :, s_ * 128:(s_ + 1) * 128],
                                             psum_bf(bank)[:, 0:CT * 128].rearrange("p (a c) -> p a c", c=128))
                    P.op("vector", ev, reads=[PSK[bank]], writes=[("xn_fm", par)])
                for s_ in range(NSUB):
                    pieces.append(lambda s_=s_: tr_one(s_))
                def store():
                    P.dma("qgps", xn_s[b, :, :, t0:t0 + BCH].rearrange("a p t -> p a t"), xn_fm[par][:],
                          reads=[("xn_fm", par)])
                pieces.append(store)
                return pieces

            def emit_ffn(i, pend):
                b, tc = i // (T // BCH), i % (T // BCH)
                par = i % 2
                def emit_down(hc):
                    hp = hc % 2
                    if hc == 0:
                        for tt in range(NSUB):
                            for ic, (co, cl) in enumerate(CCH):
                                P.op("tensor", lambda t, hc=hc, hp=hp, tt=tt, co=co, cl=cl, ic=ic:
                                     t.matmul(psum[2 + tt * NCH + ic][:, 0:cl],
                                              h_sb[hp][:, tt * 128:(tt + 1) * 128],
                                              wdT[:, hc, co:co + cl],
                                              start=True, stop=(HT == 1)),
                                     reads=[("h", hp), ("wdT", hc)],
                                     writes=[PSK[2 + tt * NCH + ic]])
                        return
                    def fnd(t, hc=hc, hp=hp):
                        for tt in range(NSUB):
                            for ic, (co, cl) in enumerate(CCH):
                                inst = t.matmul(psum[2 + tt * NCH + ic][:, 0:cl],
                                                h_sb[hp][:, tt * 128:(tt + 1) * 128],
                                                wdT[:, hc, co:co + cl],
                                                start=False, stop=(hc == HT - 1))
                        return inst
                    P.op("tensor", fnd, reads=[("h", hp), ("wdT", hc)],
                         writes=[PSK[2 + j] for j in range(NSUB * NCH)])
                for hc in range(HT):
                    hp = hc % 2
                    def fng(t, hc=hc, par=par):
                        for ct in range(CT):
                            inst = t.matmul(psum[0][:, 0:BCH], wgT[:, ct, hc * 128:(hc + 1) * 128],
                                            xn_fm[par][:, ct, :], start=(ct == 0), stop=(ct == CT - 1))
                        return inst
                    P.op("tensor", fng, reads=[("xn_fm", par), ("wgT", hc)], writes=[PSK[0]])
                    def fnu(t, hc=hc, par=par):
                        for ct in range(CT):
                            inst = t.matmul(psum[1][:, 0:BCH], wuT[:, ct, hc * 128:(hc + 1) * 128],
                                            xn_fm[par][:, ct, :], start=(ct == 0), stop=(ct == CT - 1))
                        return inst
                    P.op("tensor", fnu, reads=[("xn_fm", par), ("wuT", hc)], writes=[PSK[1]])
                    P.op("scalar", lambda s, hp=hp: s.activation(p_sb[hp][:], psum[0][:, 0:BCH], AF.Silu, scale=SW),
                         reads=[PSK[0]], writes=[("p", hp)])
                    P.op("vector", lambda v, hp=hp: v.tensor_tensor(h_sb[hp][:], p_sb[hp][:], psum[1][:, 0:BCH], ALU.mult),
                         reads=[("p", hp), PSK[1]], writes=[("h", hp)])
                    if hc == 1 and pend_g[0] is not None:
                        pend_g[0]()
                        pend_g[0] = None
                    if hc >= 1:
                        emit_down(hc - 1)
                    if pend:
                        pend.pop(0)()
                emit_down(HT - 1)
                def g_evacs(b=b, tc=tc, par=par):
                    for tt in range(NSUB):
                        for ic, (co, cl) in enumerate(CCH):
                            bank = 2 + tt * NCH + ic
                            P.op("scalar", lambda s, tt=tt, co=co, cl=cl, bank=bank, par=par:
                                 s.activation(g_sb[par][:, tt, co:co + cl], psum[bank][:, 0:cl],
                                              AF.Silu, scale=SW * SW),
                                 reads=[PSK[bank]], writes=[("g_sb", par, tt)])
                    P.dma("qgps", g_s[b, tc * NSUB:tc * NSUB + NSUB, :, :].rearrange("a p c -> p a c"),
                          g_sb[par][:], reads=[("g_sb", par, tt2) for tt2 in range(NSUB)])
                pend_g[0] = g_evacs

            pend_g = [None]
            for piece in ln_pieces(0):
                piece()
            for i in range(NBC):
                pend = ln_pieces(i + 1) if i + 1 < NBC else []
                emit_ffn(i, pend)
                for piece in pend:
                    piece()
            if pend_g[0] is not None:
                pend_g[0]()
                pend_g[0] = None

        P.barrier()

        with ExitStack() as phC:
            eC = phC.enter_context
            keff = eC(nc.sbuf_tensor("keff", [128, CT, L], BF16))
            veff = eC(nc.sbuf_tensor("veff", [128, LT, C], BF16))
            WuqT = eC(nc.sbuf_tensor("WuqT", [128, QLT, C], BF16))
            WdqT = eC(nc.sbuf_tensor("WdqT", [128, CT, QL], BF16))
            WdkvT = eC(nc.sbuf_tensor("WdkvT", [128, CT, L], BF16))
            xn_b = eC(nc.sbuf_tensor("xn_b", [128, CT, T], BF16))
            ckv_fm = eC(nc.sbuf_tensor("ckv_fm", [128, LT, T], BF16))
            ckv_tm = eC(nc.sbuf_tensor("ckv_tm", [128, TT, L], BF16))
            A_fm = eC(nc.sbuf_tensor("A_fm", [128, QLT, T], BF16))
            q2c = eC(nc.sbuf_tensor("q2c", [128, CT, QCH], BF16))
            qlat = eC(nc.sbuf_tensor("qlat", [128, LT, QCH], BF16))
            ylat = eC(nc.sbuf_tensor("ylat", [128, LT, QCH], BF16))
            exp_sb = [eC(nc.sbuf_tensor("exp%d" % i, [128, QCH], BF16)) for i in range(3)]
            s_sb = eC(nc.sbuf_tensor("s_sb", [128, NKP, C], F32))
            den_sb = eC(nc.sbuf_tensor("den_sb", [1, QCH], F32))
            rec_sb = eC(nc.sbuf_tensor("rec_sb", [128, NKP], F32))
            g_c = [eC(nc.sbuf_tensor("g_c%d" % i, [128, NKP, C], BF16)) for i in range(2)]
            xr_full = eC(nc.sbuf_tensor("xr_full", [128, NKP, C], F32))
            out_st = [eC(nc.sbuf_tensor("out_st%d" % i, [128, C], F32)) for i in range(2)]

            P.dma("qscal", keff[:], keff_s, writes=[("keff",)])
            P.dma("qscal", veff[:], veff_s, writes=[("veff",)])
            P.dma("qscal", WuqT[:], WuqT_s, writes=[("WuqT",)])
            P.dma("qscal", WdqT[:], WdqT_s, writes=[("WdqT",)])
            P.dma("qscal", WdkvT[:], WdkvT_s, writes=[("WdkvT",)])

            def load_xn(bn):
                half = T // 2
                for hh in range(2):
                    P.dma("qsync" if hh == 0 else "qscal", xn_b[:, :, hh * half:(hh + 1) * half],
                          xn_s[bn][:, :, hh * half:(hh + 1) * half].rearrange("a p t -> p a t"),
                          reads=[("xn_b", hh)], writes=[("xn_b", hh)])
            load_xn(0)
            for b in range(B):
                def proj(dst, wT, wkey, MT, dkey):
                    for mc in range(MT):
                        for ic, (to, tl) in enumerate(TCH):
                            k = mc * len(TCH) + ic
                            bank = k % 4
                            def fn(t, mc=mc, to=to, tl=tl, bank=bank, wT=wT):
                                for ct in range(CT):
                                    inst = t.matmul(psum[bank][:, 0:tl], wT[:, ct, mc * 128:(mc + 1) * 128],
                                                    xn_b[:, ct, to:to + tl],
                                                    start=(ct == 0), stop=(ct == CT - 1))
                                return inst
                            xnkeys = [("xn_b", hh) for hh in range(2)
                                      if to < (hh + 1) * (T // 2) and to + tl > hh * (T // 2)]
                            P.op("tensor", fn, reads=xnkeys + [wkey], writes=[PSK[bank]])
                            if k % 2 == 0:
                                P.op("scalar", lambda s, mc=mc, to=to, tl=tl, bank=bank, dst=dst:
                                     s.copy(dst[:, mc, to:to + tl], psum[bank][:, 0:tl]),
                                     reads=[PSK[bank]], writes=[dkey])
                            else:
                                P.op("vector", lambda v, mc=mc, to=to, tl=tl, bank=bank, dst=dst:
                                     v.tensor_copy(dst[:, mc, to:to + tl], psum[bank][:, 0:tl]),
                                     reads=[PSK[bank]], writes=[dkey])
                proj(ckv_fm, WdkvT, ("WdkvT",), LT, ("ckv_fm", b))
                proj(A_fm, WdqT, ("WdqT",), QLT, ("A_fm", b))
                for kt in range(TT):
                    bank = 2 + kt % 2
                    def fn(t, kt=kt, bank=bank):
                        for lc in range(LT):
                            inst = t.transpose(psum_bf(bank)[:, lc * 128:(lc + 1) * 128],
                                               ckv_fm[:, lc, kt * 128:(kt + 1) * 128], ident_b[:])
                        return inst
                    P.op("tensor", fn, reads=[("identb",), ("ckv_fm", b)], writes=[PSK[bank]])
                    P.op("scalar", lambda s, kt=kt, bank=bank: s.copy(ckv_tm[:, kt, 0:L], psum_bf(bank)[:, 0:L]),
                         reads=[PSK[bank]], writes=[("ckv_tm", b)])
                if b + 1 < B:
                    load_xn(b + 1)

                qlat_hoisted = [False]
                def emit_load_xr(qcn):
                    P.dma("qscal", xr_full[:], x_d[b, qcn * QCH:(qcn + 1) * QCH, :].rearrange("(j p) c -> p j c", p=128),
                          reads=[("xr",)], writes=[("xr",)])
                def emit_combine(qcn):
                    for j in range(NKP):
                        xp = j % 2
                        tq = qcn * QCH + j * 128
                        P.op("vector", lambda v, j=j, xp=xp: v.tensor_tensor(out_st[xp][:], s_sb[:, j, :], xr_full[:, j, :], ALU.add),
                             reads=[("s_sb", j), ("xr",)], writes=[("out_st", xp)])
                        P.dma("qgps", out_d[b, tq:tq + 128, :], out_st[xp][:],
                              reads=[("out_st", xp)])
                for qc in range(NQC):
                    q0 = qc * QCH
                    gpar = qc % 2
                    P.dma("qscal", g_c[gpar][:], g_s[b, qc * NKP:(qc + 1) * NKP, :, :].rearrange("a p c -> p a c"),
                          reads=[("g_c", gpar)], writes=[("g_c", gpar)])
                    def emit_q2c(qcn):
                        for cch in range(CT):
                            bank = 4 + cch % 2
                            def fn(t, cch=cch, bank=bank, qcn=qcn):
                                for qlt in range(QLT):
                                    inst = t.matmul(psum[bank][:, 0:QCH], WuqT[:, qlt, cch * 128:(cch + 1) * 128],
                                                    A_fm[:, qlt, qcn * QCH:qcn * QCH + QCH],
                                                    start=(qlt == 0), stop=(qlt == QLT - 1))
                                return inst
                            P.op("tensor", fn, reads=[("A_fm", b), ("WuqT",)], writes=[PSK[bank]])
                            if cch % 2 == 0:
                                P.op("scalar", lambda s, cch=cch, bank=bank: s.copy(q2c[:, cch, :], psum[bank][:, 0:QCH]),
                                     reads=[PSK[bank]], writes=[("q2c", cch)])
                            else:
                                P.op("vector", lambda v, cch=cch, bank=bank: v.tensor_copy(q2c[:, cch, :], psum[bank][:, 0:QCH]),
                                     reads=[PSK[bank]], writes=[("q2c", cch)])
                    if qc == 0:
                        emit_q2c(0)

                    QB = [4, 5, 7, 0]
                    ktmax = (qc + 1) * NKP

                    def emit_qlat(h):
                        hrow = (h % HPT) * HS
                        htile = h // HPT
                        def fq(t, hrow=hrow, htile=htile):
                            for lc in range(LT - 1):
                                inst = t.matmul(psum[QB[lc]][:, 0:QCH],
                                                keff[hrow:hrow + HS, htile, lc * 128:(lc + 1) * 128],
                                                q2c[hrow:hrow + HS, htile, :], start=True, stop=True)
                            return inst
                        P.op("tensor", fq, reads=[("keff",), ("q2c", htile)],
                             writes=[PSK[QB[lc2]] for lc2 in range(LT - 1)])
                        P.op("tensor", lambda t, hrow=hrow, htile=htile, lc=LT - 1:
                             t.matmul(psum[QB[lc]][:, 0:QCH],
                                      keff[hrow:hrow + HS, htile, lc * 128:(lc + 1) * 128],
                                      q2c[hrow:hrow + HS, htile, :], start=True, stop=True),
                             reads=[("keff",), ("q2c", htile)], writes=[PSK[QB[LT - 1]]])
                        for lc in range(LT):
                            if lc % 2 == 0:
                                P.op("scalar", lambda s, lc=lc: s.copy(qlat[:, lc, :], psum[QB[lc]][:, 0:QCH]),
                                     reads=[PSK[QB[lc]]], writes=[("qlat",)])
                            else:
                                P.op("vector", lambda v, lc=lc: v.tensor_copy(qlat[:, lc, :], psum[QB[lc]][:, 0:QCH]),
                                     reads=[PSK[QB[lc]]], writes=[("qlatv",)])

                    def qoff(kt):
                        mi = kt - qc * NKP
                        return mi * 128 if mi > 0 else 0

                    def emit_scores(kt):
                        sb = 4 + kt % 2
                        ep = kt % 3
                        qo = qoff(kt)
                        diag = kt >= qc * NKP
                        def fs(t, kt=kt, sb=sb, qo=qo, b=b, diag=diag):
                            for lc in range(LT):
                                inst = t.matmul(psum[sb][:, qo:QCH], ckv_fm[:, lc, kt * 128:(kt + 1) * 128],
                                                qlat[:, lc, qo:QCH], start=(lc == 0), stop=(lc == LT - 1 and not diag))
                            if diag:
                                inst = t.matmul(psum[sb][:, qo:QCH], ident_b[:], negtri[:, 0:QCH - qo],
                                                start=False, stop=True)
                            return inst
                        P.op("tensor", fs, reads=[("ckv_fm", b), ("qlat",), ("qlatv",), ("negtri",), ("identb",)],
                             writes=[PSK[sb]])
                        P.op("scalar", lambda s, sb=sb, ep=ep, qo=qo: s.activation(
                                exp_sb[ep][:, qo:QCH], psum[sb][:, qo:QCH], AF.Exp, scale=0.125),
                             reads=[PSK[sb]], writes=[("exp", ep)])

                    def emit_ylat(kt, split=False, evac=False):
                        ep = kt % 3
                        qo = qoff(kt)
                        last = (kt == ktmax - 1)
                        def emit_den():
                            P.op("tensor", lambda t, ep=ep, qo=qo, first=(kt == 0), last=last:
                                 t.matmul(psum[6][:, qo:QCH], ones_m[:], exp_sb[ep][:, qo:QCH],
                                          start=first, stop=last),
                                 reads=[("exp", ep), ("ones",)], writes=[PSK[6]], inc=last)
                        if not split:
                            def fy(t, kt=kt, ep=ep, qo=qo, b=b, first=(kt == 0), last=last):
                                for lc in range(LT):
                                    inst = t.matmul(psum[lc][:, qo:QCH], ckv_tm[:, kt, lc * 128:(lc + 1) * 128],
                                                    exp_sb[ep][:, qo:QCH], start=first, stop=last)
                                return inst
                            P.op("tensor", fy, reads=[("ckv_tm", b), ("exp", ep)],
                                 writes=[PSK[lc] for lc in range(LT)], inc=False)
                            emit_den()
                        else:
                            emit_den()
                            for lc in range(LT):
                                P.op("tensor", lambda t, kt=kt, ep=ep, qo=qo, b=b, lc=lc,
                                     first=(kt == 0), last=last:
                                     t.matmul(psum[lc][:, qo:QCH], ckv_tm[:, kt, lc * 128:(lc + 1) * 128],
                                              exp_sb[ep][:, qo:QCH], start=first, stop=last),
                                     reads=[("ckv_tm", b), ("exp", ep)], writes=[PSK[lc]])
                                if evac:
                                    if lc % 2 == 0:
                                        P.op("scalar", lambda s, lc=lc: s.copy(ylat[:, lc, :], psum[lc][:, 0:QCH]),
                                             reads=[PSK[lc]], writes=[("ylat",)])
                                    else:
                                        P.op("vector", lambda v, lc=lc: v.tensor_copy(ylat[:, lc, :], psum[lc][:, 0:QCH]),
                                             reads=[PSK[lc]], writes=[("ylatv",)])
                                    if lc == 0:
                                        P.op("scalar", lambda s: s.copy(den_sb[:], psum[6][0:1, 0:QCH]),
                                             reads=[PSK[6]], writes=[("den",)])

                    def emit_y(h):
                        for j in list(range(1, NKP)) + [0]:
                            def fyo(t, j=j, h=h):
                                for lc in range(LT):
                                    inst = t.matmul(psum[j][:, 0:HS], ylat[:, lc, j * 128:(j + 1) * 128],
                                                    veff[:, lc, h * HS:(h + 1) * HS],
                                                    start=(lc == 0), stop=(lc == LT - 1))
                                return inst
                            P.op("tensor", fyo, reads=[("ylat",), ("ylatv",), ("veff",)], writes=[PSK[j]])
                            P.op("vector", lambda v, j=j, h=h, gpar=gpar: v.scalar_tensor_tensor(
                                    s_sb[:, j, h * HS:(h + 1) * HS], psum[j][:, 0:HS],
                                    rec_sb[:, j:j + 1], g_c[gpar][:, j, h * HS:(h + 1) * HS],
                                    ALU.mult, ALU.mult),
                                 reads=[PSK[j], ("rec",), ("g_c", gpar)], writes=[("s_sb", j)])

                    def emit_fdt_recip():
                        def fdt(t):
                            for j in range(NKP):
                                inst = t.matmul(psum[6][:, j:j + 1], den_sb[0:1, j * 128:(j + 1) * 128],
                                                ones_f[0:1, 0:1], start=True, stop=True)
                            return inst
                        P.op("tensor", fdt, reads=[("den",), ("ones",)], writes=[PSK[6]])
                        P.op("vector", lambda v: v.reciprocal(rec_sb[:, 0:NKP], psum[6][:, 0:NKP]),
                             reads=[PSK[6]], writes=[("rec",)])
                    if qc == 0:
                        emit_load_xr(0)
                    prev = None
                    for h in range(NH):
                        if h > 0 or not qlat_hoisted[0]:
                            emit_qlat(h)
                        if h == 0:
                            qlat_hoisted[0] = False
                        if prev is not None:
                            emit_fdt_recip()
                            emit_y(prev)
                        for kt in range(ktmax):
                            emit_scores(kt)
                            if kt >= 1:
                                emit_ylat(kt - 1)
                        emit_ylat(ktmax - 1, split=True, evac=True)
                        prev = h
                    emit_fdt_recip()
                    if qc + 1 < NQC:
                        emit_q2c(qc + 1)
                        emit_qlat(0)
                        qlat_hoisted[0] = True
                    emit_y(prev)
                    emit_combine(qc)
                    if qc + 1 < NQC:
                        emit_load_xr(qc + 1)

            P.barrier()
            block = ent(nc.Block())
            P.emit(block, sems)
    return nc


def kernel(L0_attn_x, L0_attn_W_dq, L0_attn_W_uq, L0_attn_W_dkv, L0_attn_W_uk,
           L0_attn_W_uv, L0_attn_W_o, L0_ffn_w_gate, L0_ffn_w_up, L0_ffn_w_down):
    n = 8
    x = np.ascontiguousarray(np.asarray(L0_attn_x, dtype=np.float32))
    shards = np.split(x, n, axis=0)
    weights = {
        "W_dq": np.ascontiguousarray(np.asarray(L0_attn_W_dq, dtype=np.float32)),
        "W_uq": np.ascontiguousarray(np.asarray(L0_attn_W_uq, dtype=np.float32)),
        "W_dkv": np.ascontiguousarray(np.asarray(L0_attn_W_dkv, dtype=np.float32)),
        "W_uk": np.ascontiguousarray(np.asarray(L0_attn_W_uk, dtype=np.float32)),
        "W_uv": np.ascontiguousarray(np.asarray(L0_attn_W_uv, dtype=np.float32)),
        "W_o": np.ascontiguousarray(np.asarray(L0_attn_W_o, dtype=np.float32)),
        "w_gate": np.ascontiguousarray(np.asarray(L0_ffn_w_gate, dtype=np.int32)),
        "w_up": np.ascontiguousarray(np.asarray(L0_ffn_w_up, dtype=np.int32)),
        "w_down": np.ascontiguousarray(np.asarray(L0_ffn_w_down, dtype=np.int32)),
    }
    nc = bass.Bass("TRN2", target_bir_lowering=False)
    build(nc, CFG_FULL)
    in_maps = [dict(weights, x=shards[i]) for i in range(n)]
    res = run_bass_kernel_spmd(nc, in_maps, core_ids=list(range(n)))
    return np.concatenate([np.asarray(r["out"]) for r in res.results], axis=0)


# version 37
# speedup vs baseline: 1.0692x; 1.0389x over previous
import numpy as np
import concourse.bass as bass
import concourse.mybir as mybir
from concourse.bass_utils import run_bass_kernel_spmd
from contextlib import ExitStack

F32 = mybir.dt.float32
BF16 = mybir.dt.bfloat16
I32 = mybir.dt.int32
AF = mybir.ActivationFunctionType
ALU = mybir.AluOpType
AX = mybir.AxisListType

QCH = 512
BCH = 256

CFG_FULL = dict(B=4, T=2048, C=1024, NH=16, HS=64, L=512, QL=512, H=3072)


class Prog:

    STREAMS = ("tensor", "vector", "scalar", "gpsimd", "sync")
    QSTREAM = {"qsync": "sync", "qscal": "scalar", "qgps": "gpsimd"}
    NSLOT = 12

    def __init__(self):
        self.items = {s: [] for s in self.STREAMS}
        self.cnt = {s: 0 for s in self.STREAMS}
        self.qslot = {q: 0 for q in self.QSTREAM}
        for q in self.QSTREAM:
            for i in range(self.NSLOT):
                self.cnt[q + str(i)] = 0
        self.res = {}
        self.satisfied = {s: {} for s in self.STREAMS}
        self.max_waited = {}
        self.n_ops = 0

    def sem_names(self):
        names = list(self.STREAMS)
        for q in self.QSTREAM:
            names += [q + str(i) for i in range(self.NSLOT)]
        return names

    def _wait(self, stream, tok):
        sem, val = tok
        if sem == stream == "tensor":
            return
        if self.satisfied[stream].get(sem, 0) >= val:
            return
        self.items[stream].append(("wait", sem, val))
        self.satisfied[stream][sem] = val
        v = self.max_waited.get(sem, 0)
        if val > v:
            self.max_waited[sem] = val

    def _deps(self, stream, reads, writes):
        toks = []
        for r in reads:
            e = self.res.get(r)
            if e and e[0] is not None:
                toks.append(e[0])
        for w in writes:
            e = self.res.get(w)
            if e:
                if e[0] is not None:
                    toks.append(e[0])
                toks.extend(e[1])
        for t in sorted(set(toks)):
            self._wait(stream, t)

    def _mark(self, tok, reads, writes):
        for w in writes:
            self.res[w] = [tok, []]
        for r in reads:
            e = self.res.setdefault(r, [None, []])
            e[1].append(tok)

    def op(self, stream, fn, reads=(), writes=(), inc=True):
        self._deps(stream, reads, writes)
        if inc:
            self.cnt[stream] += 1
            tok = (stream, self.cnt[stream])
            self.items[stream].append(("op", fn, stream, 1))
        else:
            tok = (stream, self.cnt[stream] + 1)
            self.items[stream].append(("op", fn, None, 0))
        self._mark(tok, reads, writes)
        self.n_ops += 1

    def dma(self, queue, out_ap, in_ap, reads=(), writes=()):
        stream = self.QSTREAM[queue]
        slot = self.qslot[queue]
        self.qslot[queue] = (slot + 1) % self.NSLOT
        sem = queue + str(slot)
        self._wait(stream, (sem, self.cnt[sem]))
        self._deps(stream, reads, writes)
        self.cnt[sem] += 16
        tok = (sem, self.cnt[sem])
        def fn(eng, o=out_ap, i=in_ap):
            return eng.dma_start(out=o, in_=i)
        self.items[stream].append(("op", fn, sem, 16))
        self._mark(tok, reads, writes)
        self.n_ops += 1

    def barrier(self):
        snap = [(sem, val) for sem, val in self.cnt.items() if val > 0]
        for s in self.STREAMS:
            for sem, val in snap:
                self._wait(s, (sem, val))

    def finalize_check(self):
        for sem, val in self.max_waited.items():
            assert val <= self.cnt[sem], (
                f"wait on {sem} >= {val} but final count is {self.cnt[sem]} (hang)")

    def emit(self, block, sems):
        self.finalize_check()
        def mk(stream):
            def body(eng):
                for it in self.items[stream]:
                    if it[0] == "wait":
                        eng.wait_ge(sems[it[1]], it[2])
                    else:
                        inst = it[1](eng)
                        if it[2] is not None:
                            inst.then_inc(sems[it[2]], it[3])
            return body
        for stream in self.STREAMS:
            getattr(block, stream)(mk(stream))


def build(nc, cfg):
    B, T, C, NH, HS = cfg["B"], cfg["T"], cfg["C"], cfg["NH"], cfg["HS"]
    L, QL, H = cfg["L"], cfg["QL"], cfg["H"]
    CT, LT, QLT, HT, TT = C // 128, L // 128, QL // 128, H // 128, T // 128
    NQC = T // QCH
    NKP = QCH // 128
    NBC = B * T // BCH
    SW = 1.0 / 127.0
    def chunks(total, size=512):
        return [(o, min(size, total - o)) for o in range(0, total, size)]
    CCH = chunks(C)
    TCH = chunks(T)
    NCH = len(CCH)
    HPT = 128 // HS

    x_d = nc.dram_tensor("x", [B, T, C], F32, kind="ExternalInput").ap()
    Wdq_d = nc.dram_tensor("W_dq", [QL, C], F32, kind="ExternalInput").ap()
    Wuq_d = nc.dram_tensor("W_uq", [C, QL], F32, kind="ExternalInput").ap()
    Wdkv_d = nc.dram_tensor("W_dkv", [L, C], F32, kind="ExternalInput").ap()
    Wuk_d = nc.dram_tensor("W_uk", [C, L], F32, kind="ExternalInput").ap()
    Wuv_d = nc.dram_tensor("W_uv", [C, L], F32, kind="ExternalInput").ap()
    Wo_d = nc.dram_tensor("W_o", [C, C], F32, kind="ExternalInput").ap()
    wg_d = nc.dram_tensor("w_gate", [H, C], I32, kind="ExternalInput").ap()
    wu_d = nc.dram_tensor("w_up", [H, C], I32, kind="ExternalInput").ap()
    wd_d = nc.dram_tensor("w_down", [C, H], I32, kind="ExternalInput").ap()
    out_d = nc.dram_tensor("out", [B, T, C], F32, kind="ExternalOutput").ap()
    xn_s = nc.dram_tensor("xn_s", [B, CT, 128, T], BF16, kind="Internal").ap()
    g_s = nc.dram_tensor("g_s", [B, TT, 128, C], BF16, kind="Internal").ap()
    keff_s = nc.dram_tensor("keff_s", [128, CT, L], BF16, kind="Internal").ap()
    veff_s = nc.dram_tensor("veff_s", [128, LT, C], BF16, kind="Internal").ap()
    WuqT_s = nc.dram_tensor("WuqT_s", [128, QLT, C], BF16, kind="Internal").ap()
    WdqT_s = nc.dram_tensor("WdqT_s", [128, CT, QL], BF16, kind="Internal").ap()
    WdkvT_s = nc.dram_tensor("WdkvT_s", [128, CT, L], BF16, kind="Internal").ap()

    P = Prog()

    with ExitStack() as top:
        ent = top.enter_context
        sems = {}
        for name in P.sem_names():
            sems[name] = ent(nc.semaphore("sem_" + name))

        ident_b = ent(nc.sbuf_tensor("ident_b", [128, 128], BF16))
        ident_f = ent(nc.sbuf_tensor("ident_f", [128, 128], F32))
        negtri = ent(nc.sbuf_tensor("negtri", [128, QCH], BF16))
        ones_b = ent(nc.sbuf_tensor("ones_b", [128, 4], BF16))
        ones_f = ent(nc.sbuf_tensor("ones_f", [128, 4], F32))
        ones_m = ent(nc.sbuf_tensor("ones_m", [128, 128], BF16))
        iota_f = ent(nc.sbuf_tensor("iota_f", [128, QCH], F32))
        stat = ent(nc.sbuf_tensor("stat", [128, 16], F32))

        psum = [ent(nc.psum_tensor("ps%d" % i, [128, 512], F32)) for i in range(8)]
        PSK = ["PS%d" % i for i in range(8)]

        def psum_bf(i):
            return psum[i][:].bitcast(BF16)

        def mk_iota(dst, n, base):
            def fn(eng, d=dst, nn=n, bb=base):
                return eng.iota(d, [[1, nn]], base=bb, channel_multiplier=-1,
                                allow_small_or_imprecise_dtypes=True)
            return fn
        P.op("gpsimd", mk_iota(iota_f[:, 0:128], 128, 0), writes=[("io",)])
        P.op("vector", lambda v: v.tensor_single_scalar(ident_b[:], iota_f[:, 0:128], 0.0, ALU.is_equal),
             reads=[("io",)], writes=[("identb",)])
        P.op("vector", lambda v: v.tensor_single_scalar(ident_f[:], iota_f[:, 0:128], 0.0, ALU.is_equal),
             reads=[("io",)], writes=[("identf",)])
        P.op("gpsimd", mk_iota(iota_f[:], QCH, 0), writes=[("io",)])
        P.op("vector", lambda v: v.tensor_single_scalar(negtri[:], iota_f[:], 0.0, ALU.is_lt),
             reads=[("io",)], writes=[("negtri",)])
        P.op("vector", lambda v: v.tensor_scalar(negtri[:], negtri[:], -100000.0, None, ALU.mult),
             reads=[("negtri",)], writes=[("negtri",)])
        P.op("vector", lambda v: v.memset(ones_b[:], 1.0), writes=[("ones",)])
        P.op("vector", lambda v: v.memset(ones_f[:], 1.0), writes=[("ones",)])
        P.op("vector", lambda v: v.memset(ones_m[:], 1.0), writes=[("ones",)])

        with ExitStack() as ph0:
            e0 = ph0.enter_context
            Wuq_f = e0(nc.sbuf_tensor("Wuq_f", [128, CT, QL], F32))
            Wuk_f = e0(nc.sbuf_tensor("Wuk_f", [128, CT, L], F32))
            Wdq_f = e0(nc.sbuf_tensor("Wdq_f", [128, QLT, C], F32))
            Wdkv_f = e0(nc.sbuf_tensor("Wdkv_f", [128, LT, C], F32))
            Wuv_f = e0(nc.sbuf_tensor("Wuv_f", [128, CT, L], F32))
            Wo_f = e0(nc.sbuf_tensor("Wo_f", [128, CT, C], F32))
            WoT_f = e0(nc.sbuf_tensor("WoT_f", [128, CT, C], F32))
            tmp1 = e0(nc.sbuf_tensor("tmp1", [128, QLT, L], F32))
            stage_b = e0(nc.sbuf_tensor("stage_b", [128, 16, 1024], BF16))
            wevac = e0(nc.sbuf_tensor("wevac", [128, 1024], BF16))
            keff_sb = e0(nc.sbuf_tensor("keff_sb", [128, CT, L], BF16))
            veff_sb = e0(nc.sbuf_tensor("veff_sb", [128, LT, C], BF16))

            def wload(dst, src_view, key, ntile):
                hlf = max(1, ntile // 2)
                for hh, (a0, a1) in enumerate([(0, hlf), (hlf, ntile)]):
                    if a1 > a0:
                        P.dma("qsync", dst[:, a0:a1, :], src_view[:, a0:a1, :], writes=[key, (key[0], hh)])
            wload(Wuq_f, Wuq_d.rearrange("(a p) q -> p a q", p=128), ("Wuq_f",), CT)
            wload(Wuk_f, Wuk_d.rearrange("(a p) l -> p a l", p=128), ("Wuk_f",), CT)
            wload(Wdq_f, Wdq_d.rearrange("(a p) c -> p a c", p=128), ("Wdq_f",), QLT)
            wload(Wdkv_f, Wdkv_d.rearrange("(a p) c -> p a c", p=128), ("Wdkv_f",), LT)
            wload(Wuv_f, Wuv_d.rearrange("(a p) l -> p a l", p=128), ("Wuv_f",), CT)
            wload(Wo_f, Wo_d.rearrange("(a p) c -> p a c", p=128), ("Wo_f",), CT)

            for qlc in range(QLT):
                bank = qlc % 2
                def fn(t, qlc=qlc, bank=bank):
                    for ct in range(CT):
                        inst = t.matmul(psum[bank][:, 0:L], Wuq_f[:, ct, qlc * 128:(qlc + 1) * 128],
                                        Wuk_f[:, ct, 0:L], start=(ct == 0), stop=(ct == CT - 1))
                    return inst
                P.op("tensor", fn, reads=[("Wuq_f",), ("Wuk_f",)], writes=[PSK[bank]])
                P.op("scalar", lambda s, qlc=qlc, bank=bank: s.copy(tmp1[:, qlc, :], psum[bank][:, 0:L]),
                     reads=[PSK[bank]], writes=[("tmp1", qlc)])
            for cch in range(CT):
                bank = cch % 2
                def fn(t, cch=cch, bank=bank):
                    for qlt in range(QLT):
                        inst = t.matmul(psum[bank][:, 0:L], Wdq_f[:, qlt, cch * 128:(cch + 1) * 128],
                                        tmp1[:, qlt, :], start=(qlt == 0), stop=(qlt == QLT - 1))
                    return inst
                P.op("tensor", fn, reads=[("Wdq_f",)] + [("tmp1", q) for q in range(QLT)], writes=[PSK[bank]])
                P.op("scalar", lambda s, cch=cch, bank=bank: s.copy(keff_sb[:, cch, :], psum[bank][:, 0:L]),
                     reads=[PSK[bank]], writes=[("keff_sb",)])
            P.dma("qgps", keff_s, keff_sb[:], reads=[("keff_sb",)])

            def emit_T(src_f32, src_key, rows_t, cols_t, dst_dram, tag, row0=0):
                for rt in range(rows_t):
                    hh = 0 if rt < max(1, rows_t // 2) else 1
                    P.op("scalar", lambda s, rt=rt: s.copy(stage_b[:, row0 + rt, 0:cols_t * 128], src_f32[:, rt, :]),
                         reads=[(src_key[0], hh)], writes=[(tag + "stg", rt)])
                for cc in range(cols_t):
                    bank = cc % 2
                    def fn(t, cc=cc, bank=bank):
                        for rt in range(rows_t):
                            inst = t.transpose(psum_bf(bank)[:, rt * 128:(rt + 1) * 128],
                                               stage_b[:, row0 + rt, cc * 128:(cc + 1) * 128], ident_b[:])
                        return inst
                    P.op("tensor", fn, reads=[("identb",)] + [(tag + "stg", rt) for rt in range(rows_t)],
                         writes=[PSK[bank]])
                    P.op("scalar", lambda s, cc=cc, bank=bank: s.copy(wevac[:, 0:rows_t * 128],
                                                                      psum_bf(bank)[:, 0:rows_t * 128]),
                         reads=[PSK[bank]], writes=[("wevac",)])
                    P.dma("qgps", dst_dram[:, cc, :], wevac[:, 0:rows_t * 128], reads=[("wevac",)])

            emit_T(Wuq_f, ("Wuq_f",), CT, QLT, WuqT_s, "wuq", row0=0)
            emit_T(Wdq_f, ("Wdq_f",), QLT, CT, WdqT_s, "wdq", row0=CT)
            emit_T(Wdkv_f, ("Wdkv_f",), LT, CT, WdkvT_s, "wdkv", row0=CT + QLT)

            for cc in range(CT):
                for half in range((CT + 3) // 4):
                    nrt = min(4, CT - half * 4)
                    bank = (cc * 2 + half) % 2 + 2
                    def fn(t, cc=cc, half=half, bank=bank, nrt=nrt):
                        for rt in range(nrt):
                            rtg = half * 4 + rt
                            inst = t.transpose(psum[bank][:, rt * 128:(rt + 1) * 128],
                                               Wo_f[:, rtg, cc * 128:(cc + 1) * 128], ident_f[:])
                        return inst
                    wokeys = sorted({("Wo_f", 0 if (half * 4 + rt) < max(1, CT // 2) else 1)
                                     for rt in range(nrt)})
                    P.op("tensor", fn, reads=[("identf",)] + wokeys, writes=[PSK[bank]])
                    P.op("scalar", lambda s, cc=cc, half=half, bank=bank, nrt=nrt:
                         s.copy(WoT_f[:, cc, half * 512:half * 512 + nrt * 128], psum[bank][:, 0:nrt * 128]),
                         reads=[PSK[bank]], writes=[("WoT", cc)])
            for lc in range(LT):
                for ic, (co, cl) in enumerate(CCH):
                    bank = (lc + ic) % 2 + 4
                    def fn(t, lc=lc, co=co, cl=cl, bank=bank):
                        for ct in range(CT):
                            inst = t.matmul(psum[bank][:, 0:cl], Wuv_f[:, ct, lc * 128:(lc + 1) * 128],
                                            WoT_f[:, ct, co:co + cl],
                                            start=(ct == 0), stop=(ct == CT - 1))
                        return inst
                    P.op("tensor", fn, reads=[("Wuv_f",)] + [("WoT", c2) for c2 in range(CT)], writes=[PSK[bank]])
                    P.op("scalar", lambda s, lc=lc, co=co, cl=cl, bank=bank:
                         s.copy(veff_sb[:, lc, co:co + cl], psum[bank][:, 0:cl]),
                         reads=[PSK[bank]], writes=[("veff_sb",)])
            P.dma("qgps", veff_s, veff_sb[:], reads=[("veff_sb",)])

        P.barrier()

        with ExitStack() as phB:
            eB = phB.enter_context
            wgT = eB(nc.sbuf_tensor("wgT", [128, CT, H], BF16))
            wuT = eB(nc.sbuf_tensor("wuT", [128, CT, H], BF16))
            wdT = eB(nc.sbuf_tensor("wdT", [128, HT, C], BF16))
            wstg = eB(nc.sbuf_tensor("wstg", [128, 2, H], BF16))
            sq_dummy = eB(nc.sbuf_tensor("sq_dummy", [128, 1024], F32))
            x_c = [eB(nc.sbuf_tensor("x_c%d" % i, [128, BCH // 128, C], F32)) for i in range(2)]
            xn_tm = eB(nc.sbuf_tensor("xn_tm", [128, BCH // 128, C], BF16))
            xn_fm = [eB(nc.sbuf_tensor("xn_fm%d" % i, [128, CT, BCH], BF16)) for i in range(2)]
            p_sb = [eB(nc.sbuf_tensor("p_sb%d" % i, [128, BCH], BF16)) for i in range(2)]
            h_sb = [eB(nc.sbuf_tensor("h_sb%d" % i, [128, BCH], BF16)) for i in range(2)]
            g_sb = [eB(nc.sbuf_tensor("g_sb%d" % i, [128, BCH // 128, C], BF16)) for i in range(2)]

            def load_ffn(dst, src_d, tiles, tag):
                ctiles = dst.shape[1]
                for tile in range(tiles):
                    sp = tile % 2
                    sbuf = wstg[:, sp, 0:ctiles * 128]
                    P.dma("qgps", sbuf, src_d[tile * 128:(tile + 1) * 128, :],
                          reads=[("wstg", sp)], writes=[("wstg", sp)])
                    ngrp = (ctiles + 7) // 8
                    for grp in range(ngrp):
                        lo, hi = grp * 8, min(grp * 8 + 8, ctiles)
                        bank = (tile * ngrp + grp) % 2
                        def fn(t, lo=lo, hi=hi, bank=bank, sbuf=sbuf):
                            for k in range(hi - lo):
                                inst = t.transpose(psum_bf(bank)[:, k * 128:(k + 1) * 128],
                                                   sbuf[:, (lo + k) * 128:(lo + k + 1) * 128], ident_b[:])
                            return inst
                        P.op("tensor", fn, reads=[("identb",), ("wstg", sp)], writes=[PSK[bank]])
                        def ev(s, tile=tile, lo=lo, hi=hi, bank=bank, dst=dst):
                            return s.copy(dst[:, lo:hi, tile * 128:(tile + 1) * 128],
                                          psum_bf(bank)[:, 0:(hi - lo) * 128].rearrange("p (a c) -> p a c", c=128))
                        P.op("scalar", ev, reads=[PSK[bank]], writes=[(tag, tile)])

            def load_ffn_pair(dst, src_d, tag, tile0, ntile):
                ctiles = dst.shape[1]
                ncols = ctiles * 128
                sp = LF_CNT[0] % 2
                LF_CNT[0] += 1
                sbuf = wstg[:, sp, 0:ntile * ncols]
                def regions(c0, c1):
                    return [("wstg", sp, r) for r in range(c0 // C, (c1 + C - 1) // C)]
                for ti in range(ntile):
                    P.dma("qgps", sbuf[:, ti * ncols:(ti + 1) * ncols],
                          src_d[(tile0 + ti) * 128:(tile0 + ti + 1) * 128, :],
                          reads=regions(ti * ncols, (ti + 1) * ncols),
                          writes=regions(ti * ncols, (ti + 1) * ncols))
                ngrp = (ctiles + 7) // 8
                for ti in range(ntile):
                    tile = tile0 + ti
                    for grp in range(ngrp):
                        lo, hi = grp * 8, min(grp * 8 + 8, ctiles)
                        bank = LF_CNT[0] % 2
                        def fn(t, lo=lo, hi=hi, bank=bank, sbuf=sbuf, ti=ti):
                            for k in range(hi - lo):
                                inst = t.transpose(psum_bf(bank)[:, k * 128:(k + 1) * 128],
                                                   sbuf[:, ti * ncols + (lo + k) * 128: ti * ncols + (lo + k + 1) * 128],
                                                   ident_b[:])
                            return inst
                        P.op("tensor", fn,
                             reads=[("identb",)] + regions(ti * ncols + lo * 128, ti * ncols + hi * 128),
                             writes=[PSK[bank]])
                        def ev(s, tile=tile, lo=lo, hi=hi, bank=bank, dst=dst):
                            return s.copy(dst[:, lo:hi, tile * 128:(tile + 1) * 128],
                                          psum_bf(bank)[:, 0:(hi - lo) * 128].rearrange("p (a c) -> p a c", c=128))
                        P.op("scalar", ev, reads=[PSK[bank]], writes=[(tag, tile)])

            LF_CNT = [0]
            wd_next = [0]
            WG_NT = max(1, min(3, H // C))
            for tp in range(0, HT, WG_NT):
                load_ffn_pair(wgT, wg_d, "wgT", tp, min(WG_NT, HT - tp))
                load_ffn_pair(wuT, wu_d, "wuT", tp, min(WG_NT, HT - tp))
                if wd_next[0] < CT:
                    load_ffn_pair(wdT, wd_d, "wdT", wd_next[0], 1)
                    wd_next[0] += 1
            while wd_next[0] < CT:
                load_ffn_pair(wdT, wd_d, "wdT", wd_next[0], 1)
                wd_next[0] += 1

            NSUB = BCH // 128

            def ln_pieces(i):
                b, tc = i // (T // BCH), i % (T // BCH)
                t0 = tc * BCH
                par = i % 2
                xk = ("x_c", par)
                pieces = []
                def load():
                    P.dma("qsync", x_c[par][:], x_d[b, t0:t0 + BCH, :].rearrange("(s p) c -> p s c", p=128),
                          writes=[xk])
                pieces.append(load)
                def stats_v(s_):
                    P.op("vector", lambda v, par=par, s_=s_: v.reduce_sum(stat[:, s_:s_ + 1], x_c[par][:, s_, :], axis=AX.X),
                         reads=[xk], writes=[("stat",)])
                def stats_s(s_):
                    P.op("scalar", lambda s, par=par, s_=s_: s.activation(sq_dummy[:, 0:C], x_c[par][:, s_, :], AF.Square,
                                                                          accum_out=stat[:, 2 + s_:3 + s_]),
                         reads=[xk], writes=[("stat",), ("sqd",)])
                for s_ in range(NSUB):
                    pieces.append(lambda s_=s_: stats_v(s_))
                    pieces.append(lambda s_=s_: stats_s(s_))
                def meanvar():
                    P.op("vector", lambda v: v.tensor_scalar(stat[:, 4:6], stat[:, 0:2], 1.0 / C, None, ALU.mult),
                         reads=[("stat",)], writes=[("stat",)])
                    P.op("vector", lambda v: v.tensor_tensor(stat[:, 6:8], stat[:, 4:6], stat[:, 4:6], ALU.mult),
                         reads=[("stat",)], writes=[("stat",)])
                    P.op("vector", lambda v: v.scalar_tensor_tensor(stat[:, 8:10], stat[:, 2:4], 1.0 / C, stat[:, 6:8],
                                                                    ALU.mult, ALU.subtract),
                         reads=[("stat",)], writes=[("stat",)])
                    P.op("vector", lambda v: v.tensor_scalar(stat[:, 8:10], stat[:, 8:10], 1e-5, None, ALU.add),
                         reads=[("stat",)], writes=[("stat",)])
                    P.op("scalar", lambda s: s.activation(stat[:, 10:12], stat[:, 8:10], AF.Sqrt),
                         reads=[("stat",)], writes=[("stat",)])
                    P.op("vector", lambda v: v.reciprocal(stat[:, 12:14], stat[:, 10:12]),
                         reads=[("stat",)], writes=[("stat",)])
                pieces.append(meanvar)
                def xn_one(s_, hh):
                    P.op("vector", lambda v, par=par, s_=s_, hh=hh: v.tensor_scalar(
                            xn_tm[:, s_, hh * (C // 2):(hh + 1) * (C // 2)],
                            x_c[par][:, s_, hh * (C // 2):(hh + 1) * (C // 2)],
                            stat[:, 4 + s_:5 + s_], stat[:, 12 + s_:13 + s_],
                            ALU.subtract, ALU.mult),
                         reads=[xk, ("stat",)], writes=[("xn_tm", s_)])
                for s_ in range(NSUB):
                    for hh in range(2):
                        pieces.append(lambda s_=s_, hh=hh: xn_one(s_, hh))
                def tr_one(s_):
                    bank = 6 + (s_ % 2)
                    def fn(t, s_=s_, bank=bank):
                        for ct in range(CT):
                            inst = t.transpose(psum_bf(bank)[:, ct * 128:(ct + 1) * 128],
                                               xn_tm[:, s_, ct * 128:(ct + 1) * 128], ident_b[:])
                        return inst
                    P.op("tensor", fn, reads=[("identb",), ("xn_tm", s_)], writes=[PSK[bank]])
                    def ev(v, s_=s_, bank=bank, par=par):
                        return v.tensor_copy(xn_fm[par][:, :, s_ * 128:(s_ + 1) * 128],
                                             psum_bf(bank)[:, 0:CT * 128].rearrange("p (a c) -> p a c", c=128))
                    P.op("vector", ev, reads=[PSK[bank]], writes=[("xn_fm", par)])
                for s_ in range(NSUB):
                    pieces.append(lambda s_=s_: tr_one(s_))
                def store():
                    P.dma("qgps", xn_s[b, :, :, t0:t0 + BCH].rearrange("a p t -> p a t"), xn_fm[par][:],
                          reads=[("xn_fm", par)])
                pieces.append(store)
                return pieces

            def emit_ffn(i, pend):
                b, tc = i // (T // BCH), i % (T // BCH)
                par = i % 2
                def emit_down(hc):
                    hp = hc % 2
                    if hc == 0:
                        for tt in range(NSUB):
                            for ic, (co, cl) in enumerate(CCH):
                                P.op("tensor", lambda t, hc=hc, hp=hp, tt=tt, co=co, cl=cl, ic=ic:
                                     t.matmul(psum[2 + tt * NCH + ic][:, 0:cl],
                                              h_sb[hp][:, tt * 128:(tt + 1) * 128],
                                              wdT[:, hc, co:co + cl],
                                              start=True, stop=(HT == 1)),
                                     reads=[("h", hp), ("wdT", hc)],
                                     writes=[PSK[2 + tt * NCH + ic]])
                        return
                    def fnd(t, hc=hc, hp=hp):
                        for tt in range(NSUB):
                            for ic, (co, cl) in enumerate(CCH):
                                inst = t.matmul(psum[2 + tt * NCH + ic][:, 0:cl],
                                                h_sb[hp][:, tt * 128:(tt + 1) * 128],
                                                wdT[:, hc, co:co + cl],
                                                start=False, stop=(hc == HT - 1))
                        return inst
                    P.op("tensor", fnd, reads=[("h", hp), ("wdT", hc)],
                         writes=[PSK[2 + j] for j in range(NSUB * NCH)])
                for hc in range(HT):
                    hp = hc % 2
                    def fng(t, hc=hc, par=par):
                        for ct in range(CT):
                            inst = t.matmul(psum[0][:, 0:BCH], wgT[:, ct, hc * 128:(hc + 1) * 128],
                                            xn_fm[par][:, ct, :], start=(ct == 0), stop=(ct == CT - 1))
                        return inst
                    P.op("tensor", fng, reads=[("xn_fm", par), ("wgT", hc)], writes=[PSK[0]])
                    def fnu(t, hc=hc, par=par):
                        for ct in range(CT):
                            inst = t.matmul(psum[1][:, 0:BCH], wuT[:, ct, hc * 128:(hc + 1) * 128],
                                            xn_fm[par][:, ct, :], start=(ct == 0), stop=(ct == CT - 1))
                        return inst
                    P.op("tensor", fnu, reads=[("xn_fm", par), ("wuT", hc)], writes=[PSK[1]])
                    P.op("scalar", lambda s, hp=hp: s.activation(p_sb[hp][:], psum[0][:, 0:BCH], AF.Silu, scale=SW),
                         reads=[PSK[0]], writes=[("p", hp)])
                    P.op("vector", lambda v, hp=hp: v.tensor_tensor(h_sb[hp][:], p_sb[hp][:], psum[1][:, 0:BCH], ALU.mult),
                         reads=[("p", hp), PSK[1]], writes=[("h", hp)])
                    if hc == 1 and pend_g[0] is not None:
                        pend_g[0]()
                        pend_g[0] = None
                    if hc >= 1:
                        emit_down(hc - 1)
                    if pend:
                        pend.pop(0)()
                emit_down(HT - 1)
                def g_evacs(b=b, tc=tc, par=par):
                    for tt in range(NSUB):
                        for ic, (co, cl) in enumerate(CCH):
                            bank = 2 + tt * NCH + ic
                            P.op("scalar", lambda s, tt=tt, co=co, cl=cl, bank=bank, par=par:
                                 s.activation(g_sb[par][:, tt, co:co + cl], psum[bank][:, 0:cl],
                                              AF.Silu, scale=SW * SW),
                                 reads=[PSK[bank]], writes=[("g_sb", par, tt)])
                    P.dma("qgps", g_s[b, tc * NSUB:tc * NSUB + NSUB, :, :].rearrange("a p c -> p a c"),
                          g_sb[par][:], reads=[("g_sb", par, tt2) for tt2 in range(NSUB)])
                pend_g[0] = g_evacs

            pend_g = [None]
            for piece in ln_pieces(0):
                piece()
            for i in range(NBC):
                pend = ln_pieces(i + 1) if i + 1 < NBC else []
                emit_ffn(i, pend)
                for piece in pend:
                    piece()
            if pend_g[0] is not None:
                pend_g[0]()
                pend_g[0] = None

        P.barrier()

        with ExitStack() as phC:
            eC = phC.enter_context
            keff = eC(nc.sbuf_tensor("keff", [128, CT, L], BF16))
            veff = eC(nc.sbuf_tensor("veff", [128, LT, C], BF16))
            WuqT = eC(nc.sbuf_tensor("WuqT", [128, QLT, C], BF16))
            WdqT = eC(nc.sbuf_tensor("WdqT", [128, CT, QL], BF16))
            WdkvT = eC(nc.sbuf_tensor("WdkvT", [128, CT, L], BF16))
            xn_b = eC(nc.sbuf_tensor("xn_b", [128, CT, T], BF16))
            ckv_fm = eC(nc.sbuf_tensor("ckv_fm", [128, LT, T], BF16))
            ckv_tm = eC(nc.sbuf_tensor("ckv_tm", [128, TT, L], BF16))
            A_fm = eC(nc.sbuf_tensor("A_fm", [128, QLT, T], BF16))
            q2c = eC(nc.sbuf_tensor("q2c", [128, CT, QCH], BF16))
            qlat = eC(nc.sbuf_tensor("qlat", [128, LT, QCH], BF16))
            ylat = eC(nc.sbuf_tensor("ylat", [128, LT, QCH], BF16))
            exp_sb = [eC(nc.sbuf_tensor("exp%d" % i, [128, QCH], BF16)) for i in range(3)]
            s_sb = eC(nc.sbuf_tensor("s_sb", [128, NKP, C], F32))
            den_sb = eC(nc.sbuf_tensor("den_sb", [1, QCH], F32))
            rec_sb = eC(nc.sbuf_tensor("rec_sb", [128, NKP], F32))
            g_c = [eC(nc.sbuf_tensor("g_c%d" % i, [128, NKP, C], BF16)) for i in range(2)]
            xr_full = eC(nc.sbuf_tensor("xr_full", [128, NKP, C], F32))
            out_st = [eC(nc.sbuf_tensor("out_st%d" % i, [128, C], F32)) for i in range(2)]

            P.dma("qscal", keff[:], keff_s, writes=[("keff",)])
            P.dma("qscal", veff[:], veff_s, writes=[("veff",)])
            P.dma("qscal", WuqT[:], WuqT_s, writes=[("WuqT",)])
            P.dma("qscal", WdqT[:], WdqT_s, writes=[("WdqT",)])
            P.dma("qscal", WdkvT[:], WdkvT_s, writes=[("WdkvT",)])

            def load_xn(bn):
                half = T // 2
                for hh in range(2):
                    P.dma("qsync", xn_b[:, :, hh * half:(hh + 1) * half],
                          xn_s[bn][:, :, hh * half:(hh + 1) * half].rearrange("a p t -> p a t"),
                          reads=[("xn_b", hh)], writes=[("xn_b", hh)])
            load_xn(0)
            for b in range(B):
                def proj(dst, wT, wkey, MT, dkey):
                    for mc in range(MT):
                        for ic, (to, tl) in enumerate(TCH):
                            k = mc * len(TCH) + ic
                            bank = k % 4
                            def fn(t, mc=mc, to=to, tl=tl, bank=bank, wT=wT):
                                for ct in range(CT):
                                    inst = t.matmul(psum[bank][:, 0:tl], wT[:, ct, mc * 128:(mc + 1) * 128],
                                                    xn_b[:, ct, to:to + tl],
                                                    start=(ct == 0), stop=(ct == CT - 1))
                                return inst
                            xnkeys = [("xn_b", hh) for hh in range(2)
                                      if to < (hh + 1) * (T // 2) and to + tl > hh * (T // 2)]
                            P.op("tensor", fn, reads=xnkeys + [wkey], writes=[PSK[bank]])
                            if k % 2 == 0:
                                P.op("scalar", lambda s, mc=mc, to=to, tl=tl, bank=bank, dst=dst:
                                     s.copy(dst[:, mc, to:to + tl], psum[bank][:, 0:tl]),
                                     reads=[PSK[bank]], writes=[dkey])
                            else:
                                P.op("vector", lambda v, mc=mc, to=to, tl=tl, bank=bank, dst=dst:
                                     v.tensor_copy(dst[:, mc, to:to + tl], psum[bank][:, 0:tl]),
                                     reads=[PSK[bank]], writes=[dkey])
                proj(ckv_fm, WdkvT, ("WdkvT",), LT, ("ckv_fm", b))
                proj(A_fm, WdqT, ("WdqT",), QLT, ("A_fm", b))
                for kt in range(TT):
                    bank = 2 + kt % 2
                    def fn(t, kt=kt, bank=bank):
                        for lc in range(LT):
                            inst = t.transpose(psum_bf(bank)[:, lc * 128:(lc + 1) * 128],
                                               ckv_fm[:, lc, kt * 128:(kt + 1) * 128], ident_b[:])
                        return inst
                    P.op("tensor", fn, reads=[("identb",), ("ckv_fm", b)], writes=[PSK[bank]])
                    P.op("scalar", lambda s, kt=kt, bank=bank: s.copy(ckv_tm[:, kt, 0:L], psum_bf(bank)[:, 0:L]),
                         reads=[PSK[bank]], writes=[("ckv_tm", b)])
                if b + 1 < B:
                    load_xn(b + 1)

                qlat_hoisted = [False]
                def emit_load_xr(qcn):
                    P.dma("qsync", xr_full[:], x_d[b, qcn * QCH:(qcn + 1) * QCH, :].rearrange("(j p) c -> p j c", p=128),
                          reads=[("xr",)], writes=[("xr",)])
                def emit_combine(qcn):
                    for j in range(NKP):
                        xp = j % 2
                        tq = qcn * QCH + j * 128
                        P.op("vector", lambda v, j=j, xp=xp: v.tensor_tensor(out_st[xp][:], s_sb[:, j, :], xr_full[:, j, :], ALU.add),
                             reads=[("s_sb", j), ("xr",)], writes=[("out_st", xp)])
                        P.dma("qgps", out_d[b, tq:tq + 128, :], out_st[xp][:],
                              reads=[("out_st", xp)])
                for qc in range(NQC):
                    q0 = qc * QCH
                    gpar = qc % 2
                    P.dma("qsync", g_c[gpar][:], g_s[b, qc * NKP:(qc + 1) * NKP, :, :].rearrange("a p c -> p a c"),
                          reads=[("g_c", gpar)], writes=[("g_c", gpar)])
                    def emit_q2c(qcn):
                        for cch in range(CT):
                            bank = 4 + cch % 2
                            def fn(t, cch=cch, bank=bank, qcn=qcn):
                                for qlt in range(QLT):
                                    inst = t.matmul(psum[bank][:, 0:QCH], WuqT[:, qlt, cch * 128:(cch + 1) * 128],
                                                    A_fm[:, qlt, qcn * QCH:qcn * QCH + QCH],
                                                    start=(qlt == 0), stop=(qlt == QLT - 1))
                                return inst
                            P.op("tensor", fn, reads=[("A_fm", b), ("WuqT",)], writes=[PSK[bank]])
                            if cch % 2 == 0:
                                P.op("scalar", lambda s, cch=cch, bank=bank: s.copy(q2c[:, cch, :], psum[bank][:, 0:QCH]),
                                     reads=[PSK[bank]], writes=[("q2c", cch)])
                            else:
                                P.op("vector", lambda v, cch=cch, bank=bank: v.tensor_copy(q2c[:, cch, :], psum[bank][:, 0:QCH]),
                                     reads=[PSK[bank]], writes=[("q2c", cch)])
                    if qc == 0:
                        emit_q2c(0)

                    QB = [4, 5, 7, 0]
                    ktmax = (qc + 1) * NKP

                    def emit_qlat(h):
                        hrow = (h % HPT) * HS
                        htile = h // HPT
                        def fq(t, hrow=hrow, htile=htile):
                            for lc in range(LT - 1):
                                inst = t.matmul(psum[QB[lc]][:, 0:QCH],
                                                keff[hrow:hrow + HS, htile, lc * 128:(lc + 1) * 128],
                                                q2c[hrow:hrow + HS, htile, :], start=True, stop=True)
                            return inst
                        P.op("tensor", fq, reads=[("keff",), ("q2c", htile)],
                             writes=[PSK[QB[lc2]] for lc2 in range(LT - 1)])
                        P.op("tensor", lambda t, hrow=hrow, htile=htile, lc=LT - 1:
                             t.matmul(psum[QB[lc]][:, 0:QCH],
                                      keff[hrow:hrow + HS, htile, lc * 128:(lc + 1) * 128],
                                      q2c[hrow:hrow + HS, htile, :], start=True, stop=True),
                             reads=[("keff",), ("q2c", htile)], writes=[PSK[QB[LT - 1]]])
                        for lc in range(LT):
                            if lc % 2 == 0:
                                P.op("scalar", lambda s, lc=lc: s.copy(qlat[:, lc, :], psum[QB[lc]][:, 0:QCH]),
                                     reads=[PSK[QB[lc]]], writes=[("qlat",)])
                            else:
                                P.op("vector", lambda v, lc=lc: v.tensor_copy(qlat[:, lc, :], psum[QB[lc]][:, 0:QCH]),
                                     reads=[PSK[QB[lc]]], writes=[("qlatv",)])

                    def qoff(kt):
                        mi = kt - qc * NKP
                        return mi * 128 if mi > 0 else 0

                    def emit_scores(kt):
                        sb = 4 + kt % 2
                        ep = kt % 3
                        qo = qoff(kt)
                        diag = kt >= qc * NKP
                        def fs(t, kt=kt, sb=sb, qo=qo, b=b, diag=diag):
                            for lc in range(LT):
                                inst = t.matmul(psum[sb][:, qo:QCH], ckv_fm[:, lc, kt * 128:(kt + 1) * 128],
                                                qlat[:, lc, qo:QCH], start=(lc == 0), stop=(lc == LT - 1 and not diag))
                            if diag:
                                inst = t.matmul(psum[sb][:, qo:QCH], ident_b[:], negtri[:, 0:QCH - qo],
                                                start=False, stop=True)
                            return inst
                        P.op("tensor", fs, reads=[("ckv_fm", b), ("qlat",), ("qlatv",), ("negtri",), ("identb",)],
                             writes=[PSK[sb]])
                        P.op("scalar", lambda s, sb=sb, ep=ep, qo=qo: s.activation(
                                exp_sb[ep][:, qo:QCH], psum[sb][:, qo:QCH], AF.Exp, scale=0.125),
                             reads=[PSK[sb]], writes=[("exp", ep)])

                    def emit_ylat(kt, split=False, evac=False):
                        ep = kt % 3
                        qo = qoff(kt)
                        last = (kt == ktmax - 1)
                        def emit_den():
                            P.op("tensor", lambda t, ep=ep, qo=qo, first=(kt == 0), last=last:
                                 t.matmul(psum[6][:, qo:QCH], ones_m[:], exp_sb[ep][:, qo:QCH],
                                          start=first, stop=last),
                                 reads=[("exp", ep), ("ones",)], writes=[PSK[6]], inc=last)
                        if not split:
                            def fy(t, kt=kt, ep=ep, qo=qo, b=b, first=(kt == 0), last=last):
                                for lc in range(LT):
                                    inst = t.matmul(psum[lc][:, qo:QCH], ckv_tm[:, kt, lc * 128:(lc + 1) * 128],
                                                    exp_sb[ep][:, qo:QCH], start=first, stop=last)
                                return inst
                            P.op("tensor", fy, reads=[("ckv_tm", b), ("exp", ep)],
                                 writes=[PSK[lc] for lc in range(LT)], inc=False)
                            emit_den()
                        else:
                            emit_den()
                            for lc in range(LT):
                                P.op("tensor", lambda t, kt=kt, ep=ep, qo=qo, b=b, lc=lc,
                                     first=(kt == 0), last=last:
                                     t.matmul(psum[lc][:, qo:QCH], ckv_tm[:, kt, lc * 128:(lc + 1) * 128],
                                              exp_sb[ep][:, qo:QCH], start=first, stop=last),
                                     reads=[("ckv_tm", b), ("exp", ep)], writes=[PSK[lc]])
                                if evac:
                                    if lc % 2 == 0:
                                        P.op("scalar", lambda s, lc=lc: s.copy(ylat[:, lc, :], psum[lc][:, 0:QCH]),
                                             reads=[PSK[lc]], writes=[("ylat",)])
                                    else:
                                        P.op("vector", lambda v, lc=lc: v.tensor_copy(ylat[:, lc, :], psum[lc][:, 0:QCH]),
                                             reads=[PSK[lc]], writes=[("ylatv",)])
                                    if lc == 0:
                                        P.op("scalar", lambda s: s.copy(den_sb[:], psum[6][0:1, 0:QCH]),
                                             reads=[PSK[6]], writes=[("den",)])

                    def emit_y(h):
                        for j in list(range(1, NKP)) + [0]:
                            def fyo(t, j=j, h=h):
                                for lc in range(LT):
                                    inst = t.matmul(psum[j][:, 0:HS], ylat[:, lc, j * 128:(j + 1) * 128],
                                                    veff[:, lc, h * HS:(h + 1) * HS],
                                                    start=(lc == 0), stop=(lc == LT - 1))
                                return inst
                            P.op("tensor", fyo, reads=[("ylat",), ("ylatv",), ("veff",)], writes=[PSK[j]])
                            P.op("vector", lambda v, j=j, h=h, gpar=gpar: v.scalar_tensor_tensor(
                                    s_sb[:, j, h * HS:(h + 1) * HS], psum[j][:, 0:HS],
                                    rec_sb[:, j:j + 1], g_c[gpar][:, j, h * HS:(h + 1) * HS],
                                    ALU.mult, ALU.mult),
                                 reads=[PSK[j], ("rec",), ("g_c", gpar)], writes=[("s_sb", j)])

                    def emit_fdt_recip():
                        def fdt(t):
                            for j in range(NKP):
                                inst = t.matmul(psum[6][:, j:j + 1], den_sb[0:1, j * 128:(j + 1) * 128],
                                                ones_f[0:1, 0:1], start=True, stop=True)
                            return inst
                        P.op("tensor", fdt, reads=[("den",), ("ones",)], writes=[PSK[6]])
                        P.op("vector", lambda v: v.reciprocal(rec_sb[:, 0:NKP], psum[6][:, 0:NKP]),
                             reads=[PSK[6]], writes=[("rec",)])
                    if qc == 0:
                        emit_load_xr(0)
                    prev = None
                    for h in range(NH):
                        if h > 0 or not qlat_hoisted[0]:
                            emit_qlat(h)
                        if h == 0:
                            qlat_hoisted[0] = False
                        if prev is not None:
                            emit_fdt_recip()
                            emit_y(prev)
                        for kt in range(ktmax):
                            emit_scores(kt)
                            if kt >= 1:
                                emit_ylat(kt - 1)
                        emit_ylat(ktmax - 1, split=True, evac=True)
                        prev = h
                    emit_fdt_recip()
                    if qc + 1 < NQC:
                        emit_q2c(qc + 1)
                        emit_qlat(0)
                        qlat_hoisted[0] = True
                    emit_y(prev)
                    emit_combine(qc)
                    if qc + 1 < NQC:
                        emit_load_xr(qc + 1)

            P.barrier()
            block = ent(nc.Block())
            P.emit(block, sems)
    return nc


def kernel(L0_attn_x, L0_attn_W_dq, L0_attn_W_uq, L0_attn_W_dkv, L0_attn_W_uk,
           L0_attn_W_uv, L0_attn_W_o, L0_ffn_w_gate, L0_ffn_w_up, L0_ffn_w_down):
    n = 8
    x = np.ascontiguousarray(np.asarray(L0_attn_x, dtype=np.float32))
    shards = np.split(x, n, axis=0)
    weights = {
        "W_dq": np.ascontiguousarray(np.asarray(L0_attn_W_dq, dtype=np.float32)),
        "W_uq": np.ascontiguousarray(np.asarray(L0_attn_W_uq, dtype=np.float32)),
        "W_dkv": np.ascontiguousarray(np.asarray(L0_attn_W_dkv, dtype=np.float32)),
        "W_uk": np.ascontiguousarray(np.asarray(L0_attn_W_uk, dtype=np.float32)),
        "W_uv": np.ascontiguousarray(np.asarray(L0_attn_W_uv, dtype=np.float32)),
        "W_o": np.ascontiguousarray(np.asarray(L0_attn_W_o, dtype=np.float32)),
        "w_gate": np.ascontiguousarray(np.asarray(L0_ffn_w_gate, dtype=np.int32)),
        "w_up": np.ascontiguousarray(np.asarray(L0_ffn_w_up, dtype=np.int32)),
        "w_down": np.ascontiguousarray(np.asarray(L0_ffn_w_down, dtype=np.int32)),
    }
    nc = bass.Bass("TRN2", target_bir_lowering=False)
    build(nc, CFG_FULL)
    in_maps = [dict(weights, x=shards[i]) for i in range(n)]
    res = run_bass_kernel_spmd(nc, in_maps, core_ids=list(range(n)))
    return np.concatenate([np.asarray(r["out"]) for r in res.results], axis=0)


# version 38 (final)
# speedup vs baseline: 1.0778x; 1.0081x over previous
import numpy as np
import concourse.bass as bass
import concourse.mybir as mybir
from concourse.bass_utils import run_bass_kernel_spmd
from contextlib import ExitStack

F32 = mybir.dt.float32
BF16 = mybir.dt.bfloat16
I32 = mybir.dt.int32
AF = mybir.ActivationFunctionType
ALU = mybir.AluOpType
AX = mybir.AxisListType

QCH = 512
BCH = 256

CFG_FULL = dict(B=4, T=2048, C=1024, NH=16, HS=64, L=512, QL=512, H=3072)


class Prog:

    STREAMS = ("tensor", "vector", "scalar", "gpsimd", "sync")
    QSTREAM = {"qsync": "sync", "qscal": "scalar", "qgps": "gpsimd"}
    NSLOT = 8

    def __init__(self):
        self.items = {s: [] for s in self.STREAMS}
        self.cnt = {s: 0 for s in self.STREAMS}
        self.qslot = {q: 0 for q in self.QSTREAM}
        for q in self.QSTREAM:
            for i in range(self.NSLOT):
                self.cnt[q + str(i)] = 0
        self.res = {}
        self.satisfied = {s: {} for s in self.STREAMS}
        self.max_waited = {}
        self.n_ops = 0

    def sem_names(self):
        names = list(self.STREAMS)
        for q in self.QSTREAM:
            names += [q + str(i) for i in range(self.NSLOT)]
        return names

    def _wait(self, stream, tok):
        sem, val = tok
        if sem == stream == "tensor":
            return
        if self.satisfied[stream].get(sem, 0) >= val:
            return
        self.items[stream].append(("wait", sem, val))
        self.satisfied[stream][sem] = val
        v = self.max_waited.get(sem, 0)
        if val > v:
            self.max_waited[sem] = val

    def _deps(self, stream, reads, writes):
        toks = []
        for r in reads:
            e = self.res.get(r)
            if e and e[0] is not None:
                toks.append(e[0])
        for w in writes:
            e = self.res.get(w)
            if e:
                if e[0] is not None:
                    toks.append(e[0])
                toks.extend(e[1])
        for t in sorted(set(toks)):
            self._wait(stream, t)

    def _mark(self, tok, reads, writes):
        for w in writes:
            self.res[w] = [tok, []]
        for r in reads:
            e = self.res.setdefault(r, [None, []])
            e[1].append(tok)

    def op(self, stream, fn, reads=(), writes=(), inc=True):
        self._deps(stream, reads, writes)
        if inc:
            self.cnt[stream] += 1
            tok = (stream, self.cnt[stream])
            self.items[stream].append(("op", fn, stream, 1))
        else:
            tok = (stream, self.cnt[stream] + 1)
            self.items[stream].append(("op", fn, None, 0))
        self._mark(tok, reads, writes)
        self.n_ops += 1

    def dma(self, queue, out_ap, in_ap, reads=(), writes=()):
        stream = self.QSTREAM[queue]
        slot = self.qslot[queue]
        self.qslot[queue] = (slot + 1) % self.NSLOT
        sem = queue + str(slot)
        self._wait(stream, (sem, self.cnt[sem]))
        self._deps(stream, reads, writes)
        self.cnt[sem] += 16
        tok = (sem, self.cnt[sem])
        def fn(eng, o=out_ap, i=in_ap):
            return eng.dma_start(out=o, in_=i)
        self.items[stream].append(("op", fn, sem, 16))
        self._mark(tok, reads, writes)
        self.n_ops += 1

    def barrier(self):
        snap = [(sem, val) for sem, val in self.cnt.items() if val > 0]
        for s in self.STREAMS:
            for sem, val in snap:
                self._wait(s, (sem, val))

    def finalize_check(self):
        for sem, val in self.max_waited.items():
            assert val <= self.cnt[sem], (
                f"wait on {sem} >= {val} but final count is {self.cnt[sem]} (hang)")

    def emit(self, block, sems):
        self.finalize_check()
        def mk(stream):
            def body(eng):
                for it in self.items[stream]:
                    if it[0] == "wait":
                        eng.wait_ge(sems[it[1]], it[2])
                    else:
                        inst = it[1](eng)
                        if it[2] is not None:
                            inst.then_inc(sems[it[2]], it[3])
            return body
        for stream in self.STREAMS:
            getattr(block, stream)(mk(stream))


def build(nc, cfg):
    B, T, C, NH, HS = cfg["B"], cfg["T"], cfg["C"], cfg["NH"], cfg["HS"]
    L, QL, H = cfg["L"], cfg["QL"], cfg["H"]
    CT, LT, QLT, HT, TT = C // 128, L // 128, QL // 128, H // 128, T // 128
    NQC = T // QCH
    NKP = QCH // 128
    NBC = B * T // BCH
    SW = 1.0 / 127.0
    def chunks(total, size=512):
        return [(o, min(size, total - o)) for o in range(0, total, size)]
    CCH = chunks(C)
    TCH = chunks(T)
    NCH = len(CCH)
    HPT = 128 // HS

    x_d = nc.dram_tensor("x", [B, T, C], F32, kind="ExternalInput").ap()
    Wdq_d = nc.dram_tensor("W_dq", [QL, C], F32, kind="ExternalInput").ap()
    Wuq_d = nc.dram_tensor("W_uq", [C, QL], F32, kind="ExternalInput").ap()
    Wdkv_d = nc.dram_tensor("W_dkv", [L, C], F32, kind="ExternalInput").ap()
    Wuk_d = nc.dram_tensor("W_uk", [C, L], F32, kind="ExternalInput").ap()
    Wuv_d = nc.dram_tensor("W_uv", [C, L], F32, kind="ExternalInput").ap()
    Wo_d = nc.dram_tensor("W_o", [C, C], F32, kind="ExternalInput").ap()
    wg_d = nc.dram_tensor("w_gate", [H, C], I32, kind="ExternalInput").ap()
    wu_d = nc.dram_tensor("w_up", [H, C], I32, kind="ExternalInput").ap()
    wd_d = nc.dram_tensor("w_down", [C, H], I32, kind="ExternalInput").ap()
    out_d = nc.dram_tensor("out", [B, T, C], F32, kind="ExternalOutput").ap()
    xn_s = nc.dram_tensor("xn_s", [B, CT, 128, T], BF16, kind="Internal").ap()
    g_s = nc.dram_tensor("g_s", [B, TT, 128, C], BF16, kind="Internal").ap()
    keff_s = nc.dram_tensor("keff_s", [128, CT, L], BF16, kind="Internal").ap()
    veff_s = nc.dram_tensor("veff_s", [128, LT, C], BF16, kind="Internal").ap()
    WuqT_s = nc.dram_tensor("WuqT_s", [128, QLT, C], BF16, kind="Internal").ap()
    WdqT_s = nc.dram_tensor("WdqT_s", [128, CT, QL], BF16, kind="Internal").ap()
    WdkvT_s = nc.dram_tensor("WdkvT_s", [128, CT, L], BF16, kind="Internal").ap()

    P = Prog()

    with ExitStack() as top:
        ent = top.enter_context
        sems = {}
        for name in P.sem_names():
            sems[name] = ent(nc.semaphore("sem_" + name))

        ident_b = ent(nc.sbuf_tensor("ident_b", [128, 128], BF16))
        ident_f = ent(nc.sbuf_tensor("ident_f", [128, 128], F32))
        negtri = ent(nc.sbuf_tensor("negtri", [128, QCH], BF16))
        ones_b = ent(nc.sbuf_tensor("ones_b", [128, 4], BF16))
        ones_f = ent(nc.sbuf_tensor("ones_f", [128, 4], F32))
        ones_m = ent(nc.sbuf_tensor("ones_m", [128, 128], BF16))
        iota_f = ent(nc.sbuf_tensor("iota_f", [128, QCH], F32))
        stat = ent(nc.sbuf_tensor("stat", [128, 16], F32))

        psum = [ent(nc.psum_tensor("ps%d" % i, [128, 512], F32)) for i in range(8)]
        PSK = ["PS%d" % i for i in range(8)]

        def psum_bf(i):
            return psum[i][:].bitcast(BF16)

        def mk_iota(dst, n, base):
            def fn(eng, d=dst, nn=n, bb=base):
                return eng.iota(d, [[1, nn]], base=bb, channel_multiplier=-1,
                                allow_small_or_imprecise_dtypes=True)
            return fn
        P.op("gpsimd", mk_iota(iota_f[:, 0:128], 128, 0), writes=[("io",)])
        P.op("vector", lambda v: v.tensor_single_scalar(ident_b[:], iota_f[:, 0:128], 0.0, ALU.is_equal),
             reads=[("io",)], writes=[("identb",)])
        P.op("vector", lambda v: v.tensor_single_scalar(ident_f[:], iota_f[:, 0:128], 0.0, ALU.is_equal),
             reads=[("io",)], writes=[("identf",)])
        P.op("gpsimd", mk_iota(iota_f[:], QCH, 0), writes=[("io",)])
        P.op("vector", lambda v: v.tensor_single_scalar(negtri[:], iota_f[:], 0.0, ALU.is_lt),
             reads=[("io",)], writes=[("negtri",)])
        P.op("vector", lambda v: v.tensor_scalar(negtri[:], negtri[:], -100000.0, None, ALU.mult),
             reads=[("negtri",)], writes=[("negtri",)])
        P.op("vector", lambda v: v.memset(ones_b[:], 1.0), writes=[("ones",)])
        P.op("vector", lambda v: v.memset(ones_f[:], 1.0), writes=[("ones",)])
        P.op("vector", lambda v: v.memset(ones_m[:], 1.0), writes=[("ones",)])

        with ExitStack() as ph0:
            e0 = ph0.enter_context
            Wuq_f = e0(nc.sbuf_tensor("Wuq_f", [128, CT, QL], F32))
            Wuk_f = e0(nc.sbuf_tensor("Wuk_f", [128, CT, L], F32))
            Wdq_f = e0(nc.sbuf_tensor("Wdq_f", [128, QLT, C], F32))
            Wdkv_f = e0(nc.sbuf_tensor("Wdkv_f", [128, LT, C], F32))
            Wuv_f = e0(nc.sbuf_tensor("Wuv_f", [128, CT, L], F32))
            Wo_f = e0(nc.sbuf_tensor("Wo_f", [128, CT, C], F32))
            WoT_f = e0(nc.sbuf_tensor("WoT_f", [128, CT, C], F32))
            tmp1 = e0(nc.sbuf_tensor("tmp1", [128, QLT, L], F32))
            stage_b = e0(nc.sbuf_tensor("stage_b", [128, 16, 1024], BF16))
            wevac = e0(nc.sbuf_tensor("wevac", [128, 1024], BF16))
            keff_sb = e0(nc.sbuf_tensor("keff_sb", [128, CT, L], BF16))
            veff_sb = e0(nc.sbuf_tensor("veff_sb", [128, LT, C], BF16))

            def wload(dst, src_view, key, ntile):
                hlf = max(1, ntile // 2)
                for hh, (a0, a1) in enumerate([(0, hlf), (hlf, ntile)]):
                    if a1 > a0:
                        P.dma("qsync", dst[:, a0:a1, :], src_view[:, a0:a1, :], writes=[key, (key[0], hh)])
            wload(Wuq_f, Wuq_d.rearrange("(a p) q -> p a q", p=128), ("Wuq_f",), CT)
            wload(Wuk_f, Wuk_d.rearrange("(a p) l -> p a l", p=128), ("Wuk_f",), CT)
            wload(Wdq_f, Wdq_d.rearrange("(a p) c -> p a c", p=128), ("Wdq_f",), QLT)
            wload(Wdkv_f, Wdkv_d.rearrange("(a p) c -> p a c", p=128), ("Wdkv_f",), LT)
            wload(Wuv_f, Wuv_d.rearrange("(a p) l -> p a l", p=128), ("Wuv_f",), CT)
            wload(Wo_f, Wo_d.rearrange("(a p) c -> p a c", p=128), ("Wo_f",), CT)

            for qlc in range(QLT):
                bank = qlc % 2
                def fn(t, qlc=qlc, bank=bank):
                    for ct in range(CT):
                        inst = t.matmul(psum[bank][:, 0:L], Wuq_f[:, ct, qlc * 128:(qlc + 1) * 128],
                                        Wuk_f[:, ct, 0:L], start=(ct == 0), stop=(ct == CT - 1))
                    return inst
                P.op("tensor", fn, reads=[("Wuq_f",), ("Wuk_f",)], writes=[PSK[bank]])
                P.op("scalar", lambda s, qlc=qlc, bank=bank: s.copy(tmp1[:, qlc, :], psum[bank][:, 0:L]),
                     reads=[PSK[bank]], writes=[("tmp1", qlc)])
            for cch in range(CT):
                bank = cch % 2
                def fn(t, cch=cch, bank=bank):
                    for qlt in range(QLT):
                        inst = t.matmul(psum[bank][:, 0:L], Wdq_f[:, qlt, cch * 128:(cch + 1) * 128],
                                        tmp1[:, qlt, :], start=(qlt == 0), stop=(qlt == QLT - 1))
                    return inst
                P.op("tensor", fn, reads=[("Wdq_f",)] + [("tmp1", q) for q in range(QLT)], writes=[PSK[bank]])
                P.op("scalar", lambda s, cch=cch, bank=bank: s.copy(keff_sb[:, cch, :], psum[bank][:, 0:L]),
                     reads=[PSK[bank]], writes=[("keff_sb",)])
            P.dma("qgps", keff_s, keff_sb[:], reads=[("keff_sb",)])

            def emit_T(src_f32, src_key, rows_t, cols_t, dst_dram, tag, row0=0):
                for rt in range(rows_t):
                    hh = 0 if rt < max(1, rows_t // 2) else 1
                    P.op("scalar", lambda s, rt=rt: s.copy(stage_b[:, row0 + rt, 0:cols_t * 128], src_f32[:, rt, :]),
                         reads=[(src_key[0], hh)], writes=[(tag + "stg", rt)])
                for cc in range(cols_t):
                    bank = cc % 2
                    def fn(t, cc=cc, bank=bank):
                        for rt in range(rows_t):
                            inst = t.transpose(psum_bf(bank)[:, rt * 128:(rt + 1) * 128],
                                               stage_b[:, row0 + rt, cc * 128:(cc + 1) * 128], ident_b[:])
                        return inst
                    P.op("tensor", fn, reads=[("identb",)] + [(tag + "stg", rt) for rt in range(rows_t)],
                         writes=[PSK[bank]])
                    P.op("scalar", lambda s, cc=cc, bank=bank: s.copy(wevac[:, 0:rows_t * 128],
                                                                      psum_bf(bank)[:, 0:rows_t * 128]),
                         reads=[PSK[bank]], writes=[("wevac",)])
                    P.dma("qgps", dst_dram[:, cc, :], wevac[:, 0:rows_t * 128], reads=[("wevac",)])

            emit_T(Wuq_f, ("Wuq_f",), CT, QLT, WuqT_s, "wuq", row0=0)
            emit_T(Wdq_f, ("Wdq_f",), QLT, CT, WdqT_s, "wdq", row0=CT)
            emit_T(Wdkv_f, ("Wdkv_f",), LT, CT, WdkvT_s, "wdkv", row0=CT + QLT)

            for cc in range(CT):
                for half in range((CT + 3) // 4):
                    nrt = min(4, CT - half * 4)
                    bank = (cc * 2 + half) % 2 + 2
                    def fn(t, cc=cc, half=half, bank=bank, nrt=nrt):
                        for rt in range(nrt):
                            rtg = half * 4 + rt
                            inst = t.transpose(psum[bank][:, rt * 128:(rt + 1) * 128],
                                               Wo_f[:, rtg, cc * 128:(cc + 1) * 128], ident_f[:])
                        return inst
                    wokeys = sorted({("Wo_f", 0 if (half * 4 + rt) < max(1, CT // 2) else 1)
                                     for rt in range(nrt)})
                    P.op("tensor", fn, reads=[("identf",)] + wokeys, writes=[PSK[bank]])
                    P.op("scalar", lambda s, cc=cc, half=half, bank=bank, nrt=nrt:
                         s.copy(WoT_f[:, cc, half * 512:half * 512 + nrt * 128], psum[bank][:, 0:nrt * 128]),
                         reads=[PSK[bank]], writes=[("WoT", cc)])
            for lc in range(LT):
                for ic, (co, cl) in enumerate(CCH):
                    bank = (lc + ic) % 2 + 4
                    def fn(t, lc=lc, co=co, cl=cl, bank=bank):
                        for ct in range(CT):
                            inst = t.matmul(psum[bank][:, 0:cl], Wuv_f[:, ct, lc * 128:(lc + 1) * 128],
                                            WoT_f[:, ct, co:co + cl],
                                            start=(ct == 0), stop=(ct == CT - 1))
                        return inst
                    P.op("tensor", fn, reads=[("Wuv_f",)] + [("WoT", c2) for c2 in range(CT)], writes=[PSK[bank]])
                    P.op("scalar", lambda s, lc=lc, co=co, cl=cl, bank=bank:
                         s.copy(veff_sb[:, lc, co:co + cl], psum[bank][:, 0:cl]),
                         reads=[PSK[bank]], writes=[("veff_sb",)])
            P.dma("qgps", veff_s, veff_sb[:], reads=[("veff_sb",)])

        P.barrier()

        with ExitStack() as phB:
            eB = phB.enter_context
            wgT = eB(nc.sbuf_tensor("wgT", [128, CT, H], BF16))
            wuT = eB(nc.sbuf_tensor("wuT", [128, CT, H], BF16))
            wdT = eB(nc.sbuf_tensor("wdT", [128, HT, C], BF16))
            wstg = eB(nc.sbuf_tensor("wstg", [128, 2, H], BF16))
            sq_dummy = eB(nc.sbuf_tensor("sq_dummy", [128, 1024], F32))
            x_c = [eB(nc.sbuf_tensor("x_c%d" % i, [128, BCH // 128, C], F32)) for i in range(2)]
            xn_tm = eB(nc.sbuf_tensor("xn_tm", [128, BCH // 128, C], BF16))
            xn_fm = [eB(nc.sbuf_tensor("xn_fm%d" % i, [128, CT, BCH], BF16)) for i in range(2)]
            p_sb = [eB(nc.sbuf_tensor("p_sb%d" % i, [128, BCH], BF16)) for i in range(2)]
            h_sb = [eB(nc.sbuf_tensor("h_sb%d" % i, [128, BCH], BF16)) for i in range(2)]
            g_sb = [eB(nc.sbuf_tensor("g_sb%d" % i, [128, BCH // 128, C], BF16)) for i in range(2)]

            def load_ffn(dst, src_d, tiles, tag):
                ctiles = dst.shape[1]
                for tile in range(tiles):
                    sp = tile % 2
                    sbuf = wstg[:, sp, 0:ctiles * 128]
                    P.dma("qgps", sbuf, src_d[tile * 128:(tile + 1) * 128, :],
                          reads=[("wstg", sp)], writes=[("wstg", sp)])
                    ngrp = (ctiles + 7) // 8
                    for grp in range(ngrp):
                        lo, hi = grp * 8, min(grp * 8 + 8, ctiles)
                        bank = (tile * ngrp + grp) % 2
                        def fn(t, lo=lo, hi=hi, bank=bank, sbuf=sbuf):
                            for k in range(hi - lo):
                                inst = t.transpose(psum_bf(bank)[:, k * 128:(k + 1) * 128],
                                                   sbuf[:, (lo + k) * 128:(lo + k + 1) * 128], ident_b[:])
                            return inst
                        P.op("tensor", fn, reads=[("identb",), ("wstg", sp)], writes=[PSK[bank]])
                        def ev(s, tile=tile, lo=lo, hi=hi, bank=bank, dst=dst):
                            return s.copy(dst[:, lo:hi, tile * 128:(tile + 1) * 128],
                                          psum_bf(bank)[:, 0:(hi - lo) * 128].rearrange("p (a c) -> p a c", c=128))
                        P.op("scalar", ev, reads=[PSK[bank]], writes=[(tag, tile)])

            def load_ffn_pair(dst, src_d, tag, tile0, ntile):
                ctiles = dst.shape[1]
                ncols = ctiles * 128
                sp = LF_CNT[0] % 2
                LF_CNT[0] += 1
                sbuf = wstg[:, sp, 0:ntile * ncols]
                def regions(c0, c1):
                    return [("wstg", sp, r) for r in range(c0 // C, (c1 + C - 1) // C)]
                for ti in range(ntile):
                    P.dma("qgps", sbuf[:, ti * ncols:(ti + 1) * ncols],
                          src_d[(tile0 + ti) * 128:(tile0 + ti + 1) * 128, :],
                          reads=regions(ti * ncols, (ti + 1) * ncols),
                          writes=regions(ti * ncols, (ti + 1) * ncols))
                ngrp = (ctiles + 7) // 8
                for ti in range(ntile):
                    tile = tile0 + ti
                    for grp in range(ngrp):
                        lo, hi = grp * 8, min(grp * 8 + 8, ctiles)
                        bank = LF_CNT[0] % 2
                        def fn(t, lo=lo, hi=hi, bank=bank, sbuf=sbuf, ti=ti):
                            for k in range(hi - lo):
                                inst = t.transpose(psum_bf(bank)[:, k * 128:(k + 1) * 128],
                                                   sbuf[:, ti * ncols + (lo + k) * 128: ti * ncols + (lo + k + 1) * 128],
                                                   ident_b[:])
                            return inst
                        P.op("tensor", fn,
                             reads=[("identb",)] + regions(ti * ncols + lo * 128, ti * ncols + hi * 128),
                             writes=[PSK[bank]])
                        def ev(s, tile=tile, lo=lo, hi=hi, bank=bank, dst=dst):
                            return s.copy(dst[:, lo:hi, tile * 128:(tile + 1) * 128],
                                          psum_bf(bank)[:, 0:(hi - lo) * 128].rearrange("p (a c) -> p a c", c=128))
                        P.op("scalar", ev, reads=[PSK[bank]], writes=[(tag, tile)])

            LF_CNT = [0]
            wd_next = [0]
            WG_NT = max(1, min(3, H // C))
            for tp in range(0, HT, WG_NT):
                load_ffn_pair(wgT, wg_d, "wgT", tp, min(WG_NT, HT - tp))
                load_ffn_pair(wuT, wu_d, "wuT", tp, min(WG_NT, HT - tp))
                if wd_next[0] < CT:
                    load_ffn_pair(wdT, wd_d, "wdT", wd_next[0], 1)
                    wd_next[0] += 1
            while wd_next[0] < CT:
                load_ffn_pair(wdT, wd_d, "wdT", wd_next[0], 1)
                wd_next[0] += 1

            NSUB = BCH // 128

            def ln_pieces(i):
                b, tc = i // (T // BCH), i % (T // BCH)
                t0 = tc * BCH
                par = i % 2
                xk = ("x_c", par)
                pieces = []
                def load():
                    P.dma("qsync", x_c[par][:], x_d[b, t0:t0 + BCH, :].rearrange("(s p) c -> p s c", p=128),
                          writes=[xk])
                pieces.append(load)
                def stats_v(s_):
                    P.op("vector", lambda v, par=par, s_=s_: v.reduce_sum(stat[:, s_:s_ + 1], x_c[par][:, s_, :], axis=AX.X),
                         reads=[xk], writes=[("stat",)])
                def stats_s(s_):
                    P.op("scalar", lambda s, par=par, s_=s_: s.activation(sq_dummy[:, 0:C], x_c[par][:, s_, :], AF.Square,
                                                                          accum_out=stat[:, 2 + s_:3 + s_]),
                         reads=[xk], writes=[("stat",), ("sqd",)])
                for s_ in range(NSUB):
                    pieces.append(lambda s_=s_: stats_v(s_))
                    pieces.append(lambda s_=s_: stats_s(s_))
                def meanvar():
                    P.op("vector", lambda v: v.tensor_scalar(stat[:, 4:6], stat[:, 0:2], 1.0 / C, None, ALU.mult),
                         reads=[("stat",)], writes=[("stat",)])
                    P.op("vector", lambda v: v.tensor_tensor(stat[:, 6:8], stat[:, 4:6], stat[:, 4:6], ALU.mult),
                         reads=[("stat",)], writes=[("stat",)])
                    P.op("vector", lambda v: v.scalar_tensor_tensor(stat[:, 8:10], stat[:, 2:4], 1.0 / C, stat[:, 6:8],
                                                                    ALU.mult, ALU.subtract),
                         reads=[("stat",)], writes=[("stat",)])
                    P.op("vector", lambda v: v.tensor_scalar(stat[:, 8:10], stat[:, 8:10], 1e-5, None, ALU.add),
                         reads=[("stat",)], writes=[("stat",)])
                    P.op("scalar", lambda s: s.activation(stat[:, 10:12], stat[:, 8:10], AF.Sqrt),
                         reads=[("stat",)], writes=[("stat",)])
                    P.op("vector", lambda v: v.reciprocal(stat[:, 12:14], stat[:, 10:12]),
                         reads=[("stat",)], writes=[("stat",)])
                pieces.append(meanvar)
                def xn_one(s_, hh):
                    P.op("vector", lambda v, par=par, s_=s_, hh=hh: v.tensor_scalar(
                            xn_tm[:, s_, hh * (C // 2):(hh + 1) * (C // 2)],
                            x_c[par][:, s_, hh * (C // 2):(hh + 1) * (C // 2)],
                            stat[:, 4 + s_:5 + s_], stat[:, 12 + s_:13 + s_],
                            ALU.subtract, ALU.mult),
                         reads=[xk, ("stat",)], writes=[("xn_tm", s_)])
                for s_ in range(NSUB):
                    for hh in range(2):
                        pieces.append(lambda s_=s_, hh=hh: xn_one(s_, hh))
                def tr_one(s_):
                    bank = 6 + (s_ % 2)
                    def fn(t, s_=s_, bank=bank):
                        for ct in range(CT):
                            inst = t.transpose(psum_bf(bank)[:, ct * 128:(ct + 1) * 128],
                                               xn_tm[:, s_, ct * 128:(ct + 1) * 128], ident_b[:])
                        return inst
                    P.op("tensor", fn, reads=[("identb",), ("xn_tm", s_)], writes=[PSK[bank]])
                    def ev(v, s_=s_, bank=bank, par=par):
                        return v.tensor_copy(xn_fm[par][:, :, s_ * 128:(s_ + 1) * 128],
                                             psum_bf(bank)[:, 0:CT * 128].rearrange("p (a c) -> p a c", c=128))
                    P.op("vector", ev, reads=[PSK[bank]], writes=[("xn_fm", par)])
                for s_ in range(NSUB):
                    pieces.append(lambda s_=s_: tr_one(s_))
                def store():
                    P.dma("qgps", xn_s[b, :, :, t0:t0 + BCH].rearrange("a p t -> p a t"), xn_fm[par][:],
                          reads=[("xn_fm", par)])
                pieces.append(store)
                return pieces

            def emit_ffn(i, pend):
                b, tc = i // (T // BCH), i % (T // BCH)
                par = i % 2
                def emit_down(hc):
                    hp = hc % 2
                    if hc == 0:
                        for tt in range(NSUB):
                            for ic, (co, cl) in enumerate(CCH):
                                P.op("tensor", lambda t, hc=hc, hp=hp, tt=tt, co=co, cl=cl, ic=ic:
                                     t.matmul(psum[2 + tt * NCH + ic][:, 0:cl],
                                              h_sb[hp][:, tt * 128:(tt + 1) * 128],
                                              wdT[:, hc, co:co + cl],
                                              start=True, stop=(HT == 1)),
                                     reads=[("h", hp), ("wdT", hc)],
                                     writes=[PSK[2 + tt * NCH + ic]])
                        return
                    def fnd(t, hc=hc, hp=hp):
                        for tt in range(NSUB):
                            for ic, (co, cl) in enumerate(CCH):
                                inst = t.matmul(psum[2 + tt * NCH + ic][:, 0:cl],
                                                h_sb[hp][:, tt * 128:(tt + 1) * 128],
                                                wdT[:, hc, co:co + cl],
                                                start=False, stop=(hc == HT - 1))
                        return inst
                    P.op("tensor", fnd, reads=[("h", hp), ("wdT", hc)],
                         writes=[PSK[2 + j] for j in range(NSUB * NCH)])
                for hc in range(HT):
                    hp = hc % 2
                    def fng(t, hc=hc, par=par):
                        for ct in range(CT):
                            inst = t.matmul(psum[0][:, 0:BCH], wgT[:, ct, hc * 128:(hc + 1) * 128],
                                            xn_fm[par][:, ct, :], start=(ct == 0), stop=(ct == CT - 1))
                        return inst
                    P.op("tensor", fng, reads=[("xn_fm", par), ("wgT", hc)], writes=[PSK[0]])
                    def fnu(t, hc=hc, par=par):
                        for ct in range(CT):
                            inst = t.matmul(psum[1][:, 0:BCH], wuT[:, ct, hc * 128:(hc + 1) * 128],
                                            xn_fm[par][:, ct, :], start=(ct == 0), stop=(ct == CT - 1))
                        return inst
                    P.op("tensor", fnu, reads=[("xn_fm", par), ("wuT", hc)], writes=[PSK[1]])
                    P.op("scalar", lambda s, hp=hp: s.activation(p_sb[hp][:], psum[0][:, 0:BCH], AF.Silu, scale=SW),
                         reads=[PSK[0]], writes=[("p", hp)])
                    P.op("vector", lambda v, hp=hp: v.tensor_tensor(h_sb[hp][:], p_sb[hp][:], psum[1][:, 0:BCH], ALU.mult),
                         reads=[("p", hp), PSK[1]], writes=[("h", hp)])
                    if hc == 1 and pend_g[0] is not None:
                        pend_g[0]()
                        pend_g[0] = None
                    if hc >= 1:
                        emit_down(hc - 1)
                    if pend:
                        pend.pop(0)()
                emit_down(HT - 1)
                def g_evacs(b=b, tc=tc, par=par):
                    for tt in range(NSUB):
                        for ic, (co, cl) in enumerate(CCH):
                            bank = 2 + tt * NCH + ic
                            P.op("scalar", lambda s, tt=tt, co=co, cl=cl, bank=bank, par=par:
                                 s.activation(g_sb[par][:, tt, co:co + cl], psum[bank][:, 0:cl],
                                              AF.Silu, scale=SW * SW),
                                 reads=[PSK[bank]], writes=[("g_sb", par, tt)])
                    P.dma("qgps", g_s[b, tc * NSUB:tc * NSUB + NSUB, :, :].rearrange("a p c -> p a c"),
                          g_sb[par][:], reads=[("g_sb", par, tt2) for tt2 in range(NSUB)])
                pend_g[0] = g_evacs

            pend_g = [None]
            for piece in ln_pieces(0):
                piece()
            for i in range(NBC):
                pend = ln_pieces(i + 1) if i + 1 < NBC else []
                emit_ffn(i, pend)
                for piece in pend:
                    piece()
            if pend_g[0] is not None:
                pend_g[0]()
                pend_g[0] = None

        P.barrier()

        with ExitStack() as phC:
            eC = phC.enter_context
            keff = eC(nc.sbuf_tensor("keff", [128, CT, L], BF16))
            veff = eC(nc.sbuf_tensor("veff", [128, LT, C], BF16))
            WuqT = eC(nc.sbuf_tensor("WuqT", [128, QLT, C], BF16))
            WdqT = eC(nc.sbuf_tensor("WdqT", [128, CT, QL], BF16))
            WdkvT = eC(nc.sbuf_tensor("WdkvT", [128, CT, L], BF16))
            xn_b = eC(nc.sbuf_tensor("xn_b", [128, CT, T], BF16))
            ckv_fm = eC(nc.sbuf_tensor("ckv_fm", [128, LT, T], BF16))
            ckv_tm = eC(nc.sbuf_tensor("ckv_tm", [128, TT, L], BF16))
            A_fm = eC(nc.sbuf_tensor("A_fm", [128, QLT, T], BF16))
            q2c = eC(nc.sbuf_tensor("q2c", [128, CT, QCH], BF16))
            qlat = eC(nc.sbuf_tensor("qlat", [128, LT, QCH], BF16))
            ylat = eC(nc.sbuf_tensor("ylat", [128, LT, QCH], BF16))
            exp_sb = [eC(nc.sbuf_tensor("exp%d" % i, [128, QCH], BF16)) for i in range(3)]
            s_sb = eC(nc.sbuf_tensor("s_sb", [128, NKP, C], F32))
            den_sb = eC(nc.sbuf_tensor("den_sb", [1, QCH], F32))
            rec_sb = eC(nc.sbuf_tensor("rec_sb", [128, NKP], F32))
            g_c = [eC(nc.sbuf_tensor("g_c%d" % i, [128, NKP, C], BF16)) for i in range(2)]
            xr_full = eC(nc.sbuf_tensor("xr_full", [128, NKP, C], F32))
            out_st = [eC(nc.sbuf_tensor("out_st%d" % i, [128, C], F32)) for i in range(2)]

            P.dma("qscal", keff[:], keff_s, writes=[("keff",)])
            P.dma("qscal", veff[:], veff_s, writes=[("veff",)])
            P.dma("qscal", WuqT[:], WuqT_s, writes=[("WuqT",)])
            P.dma("qscal", WdqT[:], WdqT_s, writes=[("WdqT",)])
            P.dma("qscal", WdkvT[:], WdkvT_s, writes=[("WdkvT",)])

            def load_xn(bn):
                half = T // 2
                for hh in range(2):
                    P.dma("qsync", xn_b[:, :, hh * half:(hh + 1) * half],
                          xn_s[bn][:, :, hh * half:(hh + 1) * half].rearrange("a p t -> p a t"),
                          reads=[("xn_b", hh)], writes=[("xn_b", hh)])
            load_xn(0)
            for b in range(B):
                def proj(dst, wT, wkey, MT, dkey):
                    for mc in range(MT):
                        for ic, (to, tl) in enumerate(TCH):
                            k = mc * len(TCH) + ic
                            bank = k % 4
                            def fn(t, mc=mc, to=to, tl=tl, bank=bank, wT=wT):
                                for ct in range(CT):
                                    inst = t.matmul(psum[bank][:, 0:tl], wT[:, ct, mc * 128:(mc + 1) * 128],
                                                    xn_b[:, ct, to:to + tl],
                                                    start=(ct == 0), stop=(ct == CT - 1))
                                return inst
                            xnkeys = [("xn_b", hh) for hh in range(2)
                                      if to < (hh + 1) * (T // 2) and to + tl > hh * (T // 2)]
                            P.op("tensor", fn, reads=xnkeys + [wkey], writes=[PSK[bank]])
                            if k % 2 == 0:
                                P.op("scalar", lambda s, mc=mc, to=to, tl=tl, bank=bank, dst=dst:
                                     s.copy(dst[:, mc, to:to + tl], psum[bank][:, 0:tl]),
                                     reads=[PSK[bank]], writes=[dkey])
                            else:
                                P.op("vector", lambda v, mc=mc, to=to, tl=tl, bank=bank, dst=dst:
                                     v.tensor_copy(dst[:, mc, to:to + tl], psum[bank][:, 0:tl]),
                                     reads=[PSK[bank]], writes=[dkey])
                proj(ckv_fm, WdkvT, ("WdkvT",), LT, ("ckv_fm", b))
                proj(A_fm, WdqT, ("WdqT",), QLT, ("A_fm", b))
                for kt in range(TT):
                    bank = 2 + kt % 2
                    def fn(t, kt=kt, bank=bank):
                        for lc in range(LT):
                            inst = t.transpose(psum_bf(bank)[:, lc * 128:(lc + 1) * 128],
                                               ckv_fm[:, lc, kt * 128:(kt + 1) * 128], ident_b[:])
                        return inst
                    P.op("tensor", fn, reads=[("identb",), ("ckv_fm", b)], writes=[PSK[bank]])
                    P.op("scalar", lambda s, kt=kt, bank=bank: s.copy(ckv_tm[:, kt, 0:L], psum_bf(bank)[:, 0:L]),
                         reads=[PSK[bank]], writes=[("ckv_tm", b)])
                if b + 1 < B:
                    load_xn(b + 1)

                qlat_hoisted = [False]
                def emit_load_xr(qcn):
                    P.dma("qsync", xr_full[:], x_d[b, qcn * QCH:(qcn + 1) * QCH, :].rearrange("(j p) c -> p j c", p=128),
                          reads=[("xr",)], writes=[("xr",)])
                def emit_combine(qcn):
                    for j in range(NKP):
                        xp = j % 2
                        tq = qcn * QCH + j * 128
                        P.op("vector", lambda v, j=j, xp=xp: v.tensor_tensor(out_st[xp][:], s_sb[:, j, :], xr_full[:, j, :], ALU.add),
                             reads=[("s_sb", j), ("xr",)], writes=[("out_st", xp)])
                        P.dma("qgps", out_d[b, tq:tq + 128, :], out_st[xp][:],
                              reads=[("out_st", xp)])
                for qc in range(NQC):
                    q0 = qc * QCH
                    gpar = qc % 2
                    P.dma("qsync", g_c[gpar][:], g_s[b, qc * NKP:(qc + 1) * NKP, :, :].rearrange("a p c -> p a c"),
                          reads=[("g_c", gpar)], writes=[("g_c", gpar)])
                    def emit_q2c(qcn):
                        for cch in range(CT):
                            bank = 4 + cch % 2
                            def fn(t, cch=cch, bank=bank, qcn=qcn):
                                for qlt in range(QLT):
                                    inst = t.matmul(psum[bank][:, 0:QCH], WuqT[:, qlt, cch * 128:(cch + 1) * 128],
                                                    A_fm[:, qlt, qcn * QCH:qcn * QCH + QCH],
                                                    start=(qlt == 0), stop=(qlt == QLT - 1))
                                return inst
                            P.op("tensor", fn, reads=[("A_fm", b), ("WuqT",)], writes=[PSK[bank]])
                            if cch % 2 == 0:
                                P.op("scalar", lambda s, cch=cch, bank=bank: s.copy(q2c[:, cch, :], psum[bank][:, 0:QCH]),
                                     reads=[PSK[bank]], writes=[("q2c", cch)])
                            else:
                                P.op("vector", lambda v, cch=cch, bank=bank: v.tensor_copy(q2c[:, cch, :], psum[bank][:, 0:QCH]),
                                     reads=[PSK[bank]], writes=[("q2c", cch)])
                    if qc == 0:
                        emit_q2c(0)

                    QB = [4, 5, 7, 0]
                    ktmax = (qc + 1) * NKP

                    def emit_qlat(h):
                        hrow = (h % HPT) * HS
                        htile = h // HPT
                        def fq(t, hrow=hrow, htile=htile):
                            for lc in range(LT - 1):
                                inst = t.matmul(psum[QB[lc]][:, 0:QCH],
                                                keff[hrow:hrow + HS, htile, lc * 128:(lc + 1) * 128],
                                                q2c[hrow:hrow + HS, htile, :], start=True, stop=True)
                            return inst
                        P.op("tensor", fq, reads=[("keff",), ("q2c", htile)],
                             writes=[PSK[QB[lc2]] for lc2 in range(LT - 1)])
                        P.op("tensor", lambda t, hrow=hrow, htile=htile, lc=LT - 1:
                             t.matmul(psum[QB[lc]][:, 0:QCH],
                                      keff[hrow:hrow + HS, htile, lc * 128:(lc + 1) * 128],
                                      q2c[hrow:hrow + HS, htile, :], start=True, stop=True),
                             reads=[("keff",), ("q2c", htile)], writes=[PSK[QB[LT - 1]]])
                        for lc in range(LT):
                            if lc % 2 == 0:
                                P.op("scalar", lambda s, lc=lc: s.copy(qlat[:, lc, :], psum[QB[lc]][:, 0:QCH]),
                                     reads=[PSK[QB[lc]]], writes=[("qlat",)])
                            else:
                                P.op("vector", lambda v, lc=lc: v.tensor_copy(qlat[:, lc, :], psum[QB[lc]][:, 0:QCH]),
                                     reads=[PSK[QB[lc]]], writes=[("qlatv",)])

                    def qoff(kt):
                        mi = kt - qc * NKP
                        return mi * 128 if mi > 0 else 0

                    def emit_scores(kt):
                        sb = 4 + kt % 2
                        ep = kt % 3
                        qo = qoff(kt)
                        diag = kt >= qc * NKP
                        def fs(t, kt=kt, sb=sb, qo=qo, b=b, diag=diag):
                            for lc in range(LT):
                                inst = t.matmul(psum[sb][:, qo:QCH], ckv_fm[:, lc, kt * 128:(kt + 1) * 128],
                                                qlat[:, lc, qo:QCH], start=(lc == 0), stop=(lc == LT - 1 and not diag))
                            if diag:
                                inst = t.matmul(psum[sb][:, qo:qo + 128], ident_b[:], negtri[:, 0:128],
                                                start=False, stop=True)
                            return inst
                        P.op("tensor", fs, reads=[("ckv_fm", b), ("qlat",), ("qlatv",), ("negtri",), ("identb",)],
                             writes=[PSK[sb]])
                        P.op("scalar", lambda s, sb=sb, ep=ep, qo=qo: s.activation(
                                exp_sb[ep][:, qo:QCH], psum[sb][:, qo:QCH], AF.Exp, scale=0.125),
                             reads=[PSK[sb]], writes=[("exp", ep)])

                    def emit_ylat(kt, split=False, evac=False):
                        ep = kt % 3
                        qo = qoff(kt)
                        last = (kt == ktmax - 1)
                        def emit_den():
                            P.op("tensor", lambda t, ep=ep, qo=qo, first=(kt == 0), last=last:
                                 t.matmul(psum[6][:, qo:QCH], ones_m[:], exp_sb[ep][:, qo:QCH],
                                          start=first, stop=last),
                                 reads=[("exp", ep), ("ones",)], writes=[PSK[6]], inc=last)
                        if not split:
                            def fy(t, kt=kt, ep=ep, qo=qo, b=b, first=(kt == 0), last=last):
                                for lc in range(LT):
                                    inst = t.matmul(psum[lc][:, qo:QCH], ckv_tm[:, kt, lc * 128:(lc + 1) * 128],
                                                    exp_sb[ep][:, qo:QCH], start=first, stop=last)
                                return inst
                            P.op("tensor", fy, reads=[("ckv_tm", b), ("exp", ep)],
                                 writes=[PSK[lc] for lc in range(LT)], inc=False)
                            emit_den()
                        else:
                            emit_den()
                            for lc in range(LT):
                                P.op("tensor", lambda t, kt=kt, ep=ep, qo=qo, b=b, lc=lc,
                                     first=(kt == 0), last=last:
                                     t.matmul(psum[lc][:, qo:QCH], ckv_tm[:, kt, lc * 128:(lc + 1) * 128],
                                              exp_sb[ep][:, qo:QCH], start=first, stop=last),
                                     reads=[("ckv_tm", b), ("exp", ep)], writes=[PSK[lc]])
                                if evac:
                                    if lc % 2 == 0:
                                        P.op("scalar", lambda s, lc=lc: s.copy(ylat[:, lc, :], psum[lc][:, 0:QCH]),
                                             reads=[PSK[lc]], writes=[("ylat",)])
                                    else:
                                        P.op("vector", lambda v, lc=lc: v.tensor_copy(ylat[:, lc, :], psum[lc][:, 0:QCH]),
                                             reads=[PSK[lc]], writes=[("ylatv",)])
                                    if lc == 0:
                                        P.op("scalar", lambda s: s.copy(den_sb[:], psum[6][0:1, 0:QCH]),
                                             reads=[PSK[6]], writes=[("den",)])

                    def emit_y(h):
                        for j in list(range(1, NKP)) + [0]:
                            def fyo(t, j=j, h=h):
                                for lc in range(LT):
                                    inst = t.matmul(psum[j][:, 0:HS], ylat[:, lc, j * 128:(j + 1) * 128],
                                                    veff[:, lc, h * HS:(h + 1) * HS],
                                                    start=(lc == 0), stop=(lc == LT - 1))
                                return inst
                            P.op("tensor", fyo, reads=[("ylat",), ("ylatv",), ("veff",)], writes=[PSK[j]])
                            P.op("vector", lambda v, j=j, h=h, gpar=gpar: v.scalar_tensor_tensor(
                                    s_sb[:, j, h * HS:(h + 1) * HS], psum[j][:, 0:HS],
                                    rec_sb[:, j:j + 1], g_c[gpar][:, j, h * HS:(h + 1) * HS],
                                    ALU.mult, ALU.mult),
                                 reads=[PSK[j], ("rec",), ("g_c", gpar)], writes=[("s_sb", j)])

                    def emit_fdt_recip():
                        def fdt(t):
                            for j in range(NKP):
                                inst = t.matmul(psum[6][:, j:j + 1], den_sb[0:1, j * 128:(j + 1) * 128],
                                                ones_f[0:1, 0:1], start=True, stop=True)
                            return inst
                        P.op("tensor", fdt, reads=[("den",), ("ones",)], writes=[PSK[6]])
                        P.op("vector", lambda v: v.reciprocal(rec_sb[:, 0:NKP], psum[6][:, 0:NKP]),
                             reads=[PSK[6]], writes=[("rec",)])
                    if qc == 0:
                        emit_load_xr(0)
                    prev = None
                    for h in range(NH):
                        if h > 0 or not qlat_hoisted[0]:
                            emit_qlat(h)
                        if h == 0:
                            qlat_hoisted[0] = False
                        if prev is not None:
                            emit_fdt_recip()
                            emit_y(prev)
                        for kt in range(ktmax):
                            emit_scores(kt)
                            if kt >= 1:
                                emit_ylat(kt - 1)
                        emit_ylat(ktmax - 1, split=True, evac=True)
                        prev = h
                    emit_fdt_recip()
                    if qc + 1 < NQC:
                        emit_q2c(qc + 1)
                        emit_qlat(0)
                        qlat_hoisted[0] = True
                    emit_y(prev)
                    emit_combine(qc)
                    if qc + 1 < NQC:
                        emit_load_xr(qc + 1)

            P.barrier()
            block = ent(nc.Block())
            P.emit(block, sems)
    return nc


def kernel(L0_attn_x, L0_attn_W_dq, L0_attn_W_uq, L0_attn_W_dkv, L0_attn_W_uk,
           L0_attn_W_uv, L0_attn_W_o, L0_ffn_w_gate, L0_ffn_w_up, L0_ffn_w_down):
    n = 8
    x = np.ascontiguousarray(np.asarray(L0_attn_x, dtype=np.float32))
    shards = np.split(x, n, axis=0)
    weights = {
        "W_dq": np.ascontiguousarray(np.asarray(L0_attn_W_dq, dtype=np.float32)),
        "W_uq": np.ascontiguousarray(np.asarray(L0_attn_W_uq, dtype=np.float32)),
        "W_dkv": np.ascontiguousarray(np.asarray(L0_attn_W_dkv, dtype=np.float32)),
        "W_uk": np.ascontiguousarray(np.asarray(L0_attn_W_uk, dtype=np.float32)),
        "W_uv": np.ascontiguousarray(np.asarray(L0_attn_W_uv, dtype=np.float32)),
        "W_o": np.ascontiguousarray(np.asarray(L0_attn_W_o, dtype=np.float32)),
        "w_gate": np.ascontiguousarray(np.asarray(L0_ffn_w_gate, dtype=np.int32)),
        "w_up": np.ascontiguousarray(np.asarray(L0_ffn_w_up, dtype=np.int32)),
        "w_down": np.ascontiguousarray(np.asarray(L0_ffn_w_down, dtype=np.int32)),
    }
    nc = bass.Bass("TRN2", target_bir_lowering=False)
    build(nc, CFG_FULL)
    in_maps = [dict(weights, x=shards[i]) for i in range(n)]
    res = run_bass_kernel_spmd(nc, in_maps, core_ids=list(range(n)))
    return np.concatenate([np.asarray(r["out"]) for r in res.results], axis=0)
